# version 17
# baseline (speedup 1.0000x reference)
import numpy as np
import ml_dtypes
from contextlib import ExitStack
import concourse.bass as bass
import concourse.mybir as mybir
from concourse.bass_utils import run_bass_kernel_spmd

F32 = mybir.dt.float32
BF16 = mybir.dt.bfloat16
AF = mybir.ActivationFunctionType
ALU = mybir.AluOpType
AX = mybir.AxisListType
BF = ml_dtypes.bfloat16

S = 2048
E = 2048
H = 16
HD = 128
DH = 8192
NT = E // 128
SCALER = float(HD) ** -0.5
CSHIFT = 6.0
EPS = 1e-5
ILAM = 0.8
VSEC = HD + 1
VROW = H * VSEC


class Sched:
    def __init__(self, nc):
        self.nc = nc
        self.ops = {k: [] for k in ("sync", "gpsimd", "tensor", "vector", "scalar")}
        self.cnt = {}
        self.sems = {}

    def sem(self, stk, name):
        self.sems[name] = stk.enter_context(self.nc.semaphore(name))
        self.cnt[name] = 0
        return name

    def emit(self, engine, build, waits=(), inc=None):
        tok = None
        if inc is not None:
            self.cnt[inc] += 16 if inc.startswith("d") else 1
            tok = self.cnt[inc]
        self.ops[engine].append((tuple((s, v) for (s, v) in waits if v and v > 0),
                                 build, inc, tok))
        return tok

    def replay(self, engine_name, eng):
        hwm = {}
        for waits, build, inc, tok in self.ops[engine_name]:
            for s, v in waits:
                if hwm.get(s, 0) >= v:
                    continue
                eng.wait_ge(self.sems[s], v)
                hwm[s] = v
            ins = build(eng)
            if inc is not None:
                ins.then_inc(self.sems[inc], 16 if inc.startswith("d") else 1)


def _consts():
    ident = np.eye(128, dtype=BF)
    PT = np.zeros((128, 128), dtype=np.float32)
    for e in range(64):
        PT[e + 64, e] = -1.0
        PT[e, e + 64] = 1.0
    ones_col = np.ones((128, 1), dtype=BF)
    ones_row_f = np.ones((1, 128), dtype=np.float32)
    inv = 1.0 / 10000.0 ** (np.arange(0, HD, 2, dtype=np.float64) / HD)
    ang = np.arange(S, dtype=np.float64)[:, None] * inv[None, :]
    ang = np.concatenate([ang, ang], axis=-1)
    cosT = np.cos(ang).T.astype(np.float32)
    sinT = np.sin(ang).T.astype(np.float32)
    e1 = float(np.e)
    m0 = np.where(np.arange(128)[:, None] > np.arange(256)[None, :], e1, 1.0).astype(BF)
    m1 = np.where(np.arange(128)[:, None] + 128 > np.arange(256)[None, :], e1, 1.0).astype(BF)
    return ident, PT.astype(BF), ones_col, ones_row_f, cosT, sinT, m0, m1


def build_program(debug=False):
    nc = bass.Bass("TRN2", target_bir_lowering=False)

    xT = nc.dram_tensor("xT", [E, S], F32, kind="ExternalInput").ap()
    wqkvT = nc.dram_tensor("wqkvT", [E, 3 * E], F32, kind="ExternalInput").ap()
    woT = nc.dram_tensor("woT", [E, E], F32, kind="ExternalInput").ap()
    wgT = nc.dram_tensor("wgT", [E, DH], F32, kind="ExternalInput").ap()
    wuT = nc.dram_tensor("wuT", [E, DH], F32, kind="ExternalInput").ap()
    wdT = nc.dram_tensor("wdT", [DH, E], F32, kind="ExternalInput").ap()
    gam_d = nc.dram_tensor("gam", [1, S], F32, kind="ExternalInput").ap()
    bet_d = nc.dram_tensor("bet", [1, S], F32, kind="ExternalInput").ap()
    lam_d = {k: nc.dram_tensor(k, [1, HD], F32, kind="ExternalInput").ap()
             for k in ("lq1", "lq2", "lk1", "lk2")}
    yout = nc.dram_tensor("y", [E, S], F32, kind="ExternalOutput").ap()

    scrkind = "ExternalOutput" if debug else "Internal"
    qk_scr = nc.dram_tensor("qk_scr", [2 * H * 128, S], BF16, kind=scrkind).ap()
    gated_scr = nc.dram_tensor("gated_scr", [DH, S], BF16, kind=scrkind).ap()
    attn_scr = nc.dram_tensor("attn_scr", [E, S], BF16, kind=scrkind).ap()
    yt_scr = nc.dram_tensor("yt_scr", [H * 128, S], BF16,
                            kind=("ExternalOutput" if debug else "Internal")).ap()
    wob_scr = nc.dram_tensor("wob_scr", [16, 128, NT * 128], BF16, kind="Internal").ap()
    wdb_scr = nc.dram_tensor("wdb_scr", [16, 128, 64 * 128], BF16, kind="Internal").ap()
    if debug:
        xn_d = nc.dram_tensor("xn_d", [128, NT * S], BF16, kind="ExternalOutput").ap()
        vall_d = nc.dram_tensor("vall_d", [128, 16 * VROW], BF16, kind="ExternalOutput").ap()
        yt_d = nc.dram_tensor("yt_d", [128, H * S], BF16, kind="ExternalOutput").ap()
        ytn_d = nc.dram_tensor("ytn_d", [128, H * S], BF16, kind="ExternalOutput").ap()
        mb_d = nc.dram_tensor("mb_d", [128, S], F32, kind="ExternalOutput").ap()
        rb_d = nc.dram_tensor("rb_d", [128, S], F32, kind="ExternalOutput").ap()
        lam_dbg = nc.dram_tensor("lam_dbg", [128, 1], F32, kind="ExternalOutput").ap()
        lsc_dbg = nc.dram_tensor("lsc_dbg", [1, 8], F32, kind="ExternalOutput").ap()
        xn_d2 = nc.dram_tensor("xn_d2", [128, NT * S], BF16, kind="ExternalOutput").ap()
        mb_d2 = nc.dram_tensor("mb_d2", [128, S], F32, kind="ExternalOutput").ap()
        rb_d2 = nc.dram_tensor("rb_d2", [128, S], F32, kind="ExternalOutput").ap()
        xr_d = nc.dram_tensor("xr_d", [128, 2 * S], F32, kind="ExternalOutput").ap()

    c_id, c_PT, c_oc, c_orf, c_cos, c_sin, c_m0, c_m1 = _consts()
    dident = nc.inline_tensor(c_id, "c_ident").ap()
    dPT = nc.inline_tensor(c_PT, "c_PT").ap()
    doc = nc.inline_tensor(c_oc, "c_onescol").ap()
    dorf = nc.inline_tensor(c_orf, "c_onesrowf").ap()
    dcos = nc.inline_tensor(c_cos, "c_cos").ap()
    dsin = nc.inline_tensor(c_sin, "c_sin").ap()
    dm0 = nc.inline_tensor(c_m0, "c_m0").ap()
    dm1 = nc.inline_tensor(c_m1, "c_m1").ap()

    stk = ExitStack()
    S_ = Sched(nc)
    sem = lambda n: S_.sem(stk, n)
    sMM, sACT, sDVE = sem("sMM"), sem("sACT"), sem("sDVE")
    dCONST = sem("dCONST")
    dX = [sem("dX0"), sem("dX1")]
    dWV = [sem("dWV0"), sem("dWV1")]
    dWQ = [sem("dWQ0"), sem("dWQ1")]
    dWK = [sem("dWK0"), sem("dWK1")]
    dQS = [sem("dQS0"), sem("dQS1")]
    dWG = [sem("dWG0"), sem("dWG1")]
    dWU = [sem("dWU0"), sem("dWU1")]
    dGS = [sem("dGS0"), sem("dGS1")]
    dQL = [sem("dQL0"), sem("dQL1")]
    dKL = [sem("dKL0"), sem("dKL1")]
    dWO = [sem("dWO0"), sem("dWO1")]
    dAS = [sem("dAS0"), sem("dAS1")]
    dG5 = sem("dG5")
    dWD = [sem("dWD0"), sem("dWD1")]
    dT = [sem("dT0"), sem("dT1"), sem("dT2"), sem("dT3")]
    dWC = sem("dWC")
    dYT = sem("dYT")
    dYN = sem("dYN")
    dOS = [sem("dOS0"), sem("dOS1")]
    dDBG = sem("dDBG")

    def gate():
        return [(sMM, S_.cnt[sMM]), (sACT, S_.cnt[sACT]), (sDVE, S_.cnt[sDVE])]

    def sb(scope, name, shape, dt):
        return scope.enter_context(nc.sbuf_tensor(name, shape, dt)).ap()

    def ps(scope, name, shape, dt=F32):
        return scope.enter_context(nc.psum_tensor(name, shape, dt)).ap()

    ident_s = sb(stk, "ident_s", [128, 128], BF16)
    PT_s = sb(stk, "PT_s", [128, 128], BF16)
    onescol_s = sb(stk, "onescol_s", [128, 1], BF16)
    onesrowf_s = sb(stk, "onesrowf_s", [1, 128], F32)
    m0_s = sb(stk, "m0_s", [128, 256], BF16)
    m1_s = sb(stk, "m1_s", [128, 256], BF16)
    eps_b = sb(stk, "eps_b", [1, 1], F32)
    bias_n = sb(stk, "bias_n", [128, 1], F32)
    bias_f = sb(stk, "bias_f", [128, 1], F32)
    lamneg_b = sb(stk, "lamneg_b", [128, 1], F32)

    for dst, src in ((ident_s, dident), (PT_s, dPT), (onescol_s, doc),
                     (onesrowf_s, dorf), (m0_s, dm0), (m1_s, dm1)):
        S_.emit("sync", lambda e, d=dst, s=src: e.dma_start(out=d, in_=s), inc=dCONST)
    tok_consts = S_.cnt[dCONST]
    S_.emit("vector", lambda e: e.memset(eps_b, EPS), inc=sDVE)
    S_.emit("vector", lambda e: e.memset(bias_n, -CSHIFT), inc=sDVE)
    tok_bias = S_.emit("vector", lambda e: e.memset(bias_f, -CSHIFT + 1.0), inc=sDVE)

    with ExitStack() as big:
        v_all = sb(big, "v_all", [128, 16 * VROW], BF16)
        v4 = v_all.rearrange("p (jb h d) -> p jb h d", h=H, d=VSEC)
        tok_vones = S_.emit("vector", lambda e: e.memset(v4[:, :, :, HD:HD + 1], 1.0),
                            inc=sDVE)

        with ExitStack() as xnscope:
            xn = sb(xnscope, "xn", [128, NT * S], BF16)

            with ExitStack() as p0:
                xring = [sb(p0, "xr0", [128, S], F32), sb(p0, "xr1", [128, S], F32)]
                m_row = sb(p0, "m_row", [1, S], F32)
                r_row = sb(p0, "r_row", [1, S], F32)
                trow = sb(p0, "trow", [1, S], F32)

                with ExitStack() as p0a:
                    xbf = sb(p0a, "xbf", [128, S], BF16)
                    xsq = sb(p0a, "xsq", [128, S], BF16)
                    ps_sum = ps(p0a, "ps_sum", [1, S])
                    ps_sq = ps(p0a, "ps_sq", [1, S])

                    ctok = [0] * NT
                    qtok = [0] * NT
                    mtok = [0] * NT
                    xtok = [0] * NT
                    for t in range(NT):
                        w = [(sACT, ctok[t - 2])] if t >= 2 else []
                        xtok[t] = S_.emit(
                            "sync",
                            lambda e, t=t: e.dma_start(out=xring[t % 2],
                                                       in_=xT[t * 128:(t + 1) * 128, :]),
                            waits=w, inc=dX[t % 2])
                        w = [(dX[t % 2], xtok[t])]
                        if t >= 1:
                            w += [(sMM, mtok[t - 1]), (sDVE, qtok[t - 1])]
                        ctok[t] = S_.emit(
                            "scalar", lambda e, t=t: e.copy(out=xbf, in_=xring[t % 2]),
                            waits=w, inc=sACT)
                        w = [(sACT, ctok[t])]
                        if t >= 1:
                            w.append((sMM, mtok[t - 1]))
                        qtok[t] = S_.emit(
                            "vector",
                            lambda e: e.tensor_tensor(out=xsq, in0=xbf, in1=xbf,
                                                      op=ALU.mult),
                            waits=w, inc=sDVE)

                        def mmstats(e, t=t):
                            ins = None
                            for k in range(4):
                                ins = e.matmul(ps_sum[:, k * 512:(k + 1) * 512], onescol_s,
                                               xbf[:, k * 512:(k + 1) * 512],
                                               start=(t == 0), stop=(t == NT - 1))
                            for k in range(4):
                                ins = e.matmul(ps_sq[:, k * 512:(k + 1) * 512], onescol_s,
                                               xsq[:, k * 512:(k + 1) * 512],
                                               start=(t == 0), stop=(t == NT - 1))
                            return ins
                        mtok[t] = S_.emit("tensor", mmstats,
                                          waits=[(dCONST, tok_consts), (sACT, ctok[t]),
                                                 (sDVE, qtok[t])], inc=sMM)
                    tok_stats = mtok[NT - 1]
                    tmr = S_.emit("vector",
                                  lambda e: e.tensor_scalar_mul(out=m_row, in0=ps_sum,
                                                                scalar1=1.0 / E),
                                  waits=[(sMM, tok_stats)], inc=sDVE)
                    S_.emit("vector",
                            lambda e: e.tensor_tensor(out=trow, in0=m_row, in1=m_row,
                                                      op=ALU.mult),
                            waits=[(sDVE, tmr)], inc=sDVE)
                    tv = S_.emit("vector",
                                 lambda e: e.scalar_tensor_tensor(
                                     out=trow, in0=ps_sq, scalar=1.0 / E, in1=trow,
                                     op0=ALU.mult, op1=ALU.subtract),
                                 waits=[(sMM, tok_stats)], inc=sDVE)
                    S_.emit("scalar",
                            lambda e: e.activation(out=trow, in_=trow, func=AF.Ln,
                                                   bias=eps_b[0:1, :]),
                            waits=[(sDVE, tv)], inc=sACT)
                    t_r = S_.emit("scalar",
                                  lambda e: e.activation(out=r_row, in_=trow, func=AF.Exp,
                                                         scale=-0.5), inc=sACT)

                with ExitStack() as p0b:
                    m_b = sb(p0b, "m_b", [128, S], F32)
                    r_b = sb(p0b, "r_b", [128, S], F32)
                    lamrow = {k: sb(p0b, k + "_s", [1, HD], F32)
                              for k in ("lq1", "lq2", "lk1", "lk2")}
                    ltmp = sb(p0b, "ltmp", [1, HD], F32)
                    lsc = sb(p0b, "lsc", [1, 8], F32)
                    ps_bc = ps(p0b, "ps_bc", [128, S])
                    ps_lam = ps(p0b, "ps_lam", [128, 1])
                    gk = gate()
                    for k in ("lq1", "lq2", "lk1", "lk2"):
                        S_.emit("sync",
                                lambda e, d=lamrow[k], s=lam_d[k]: e.dma_start(out=d,
                                                                               in_=s),
                                waits=gk, inc=dCONST)
                        gk = ()
                    tok_lrows = S_.cnt[dCONST]

                    def bc(e, src):
                        ins = None
                        for k in range(4):
                            ins = e.matmul(ps_bc[:, k * 512:(k + 1) * 512], onesrowf_s,
                                           src[:, k * 512:(k + 1) * 512],
                                           start=True, stop=True)
                        return ins
                    t1 = S_.emit("tensor", lambda e: bc(e, m_row),
                                 waits=[(sACT, t_r), (sDVE, tmr)], inc=sMM)
                    t2 = S_.emit("scalar", lambda e: e.copy(out=m_b, in_=ps_bc),
                                 waits=[(sMM, t1)], inc=sACT)
                    t3 = S_.emit("tensor", lambda e: bc(e, r_row),
                                 waits=[(sACT, t2)], inc=sMM)
                    t_rb = S_.emit("scalar", lambda e: e.copy(out=r_b, in_=ps_bc),
                                   waits=[(sMM, t3)], inc=sACT)

                    c0 = S_.emit("vector",
                                 lambda e: e.tensor_tensor(out=ltmp, in0=lamrow["lq1"],
                                                           in1=lamrow["lk1"],
                                                           op=ALU.mult),
                                 waits=[(dCONST, tok_lrows)], inc=sDVE)
                    c1 = S_.emit("vector",
                                 lambda e: e.reduce_sum(out=lsc[:, 0:1], in_=ltmp,
                                                        axis=AX.X),
                                 waits=[(sDVE, c0)], inc=sDVE)
                    c2 = S_.emit("vector",
                                 lambda e: e.tensor_tensor(out=ltmp, in0=lamrow["lq2"],
                                                           in1=lamrow["lk2"],
                                                           op=ALU.mult),
                                 waits=[(sDVE, c1)], inc=sDVE)
                    c3 = S_.emit("vector",
                                 lambda e: e.reduce_sum(out=lsc[:, 1:2], in_=ltmp,
                                                        axis=AX.X),
                                 waits=[(sDVE, c2)], inc=sDVE)
                    c4 = S_.emit("scalar",
                                 lambda e: e.activation(out=lsc[:, 2:4], in_=lsc[:, 0:2],
                                                        func=AF.Exp),
                                 waits=[(sDVE, c3)], inc=sACT)
                    c5 = S_.emit("vector",
                                 lambda e: e.tensor_tensor(out=lsc[:, 4:5],
                                                           in0=lsc[:, 2:3],
                                                           in1=lsc[:, 3:4],
                                                           op=ALU.subtract),
                                 waits=[(sACT, c4)], inc=sDVE)
                    c6 = S_.emit("vector",
                                 lambda e: e.tensor_scalar_add(out=lsc[:, 5:6],
                                                               in0=lsc[:, 4:5],
                                                               scalar1=ILAM),
                                 waits=[(sDVE, c5)], inc=sDVE)
                    c7 = S_.emit("vector",
                                 lambda e: e.tensor_scalar_mul(out=lsc[:, 6:7],
                                                               in0=lsc[:, 5:6],
                                                               scalar1=-1.0),
                                 waits=[(sDVE, c6)], inc=sDVE)
                    c8 = S_.emit("tensor",
                                 lambda e: e.matmul(ps_lam, onesrowf_s, lsc[:, 6:7],
                                                    start=True, stop=True),
                                 waits=[(sDVE, c7)], inc=sMM)
                    tok_lam = S_.emit("scalar", lambda e: e.copy(out=lamneg_b, in_=ps_lam),
                                      waits=[(sMM, c8)], inc=sACT)
                    if debug:
                        S_.emit("sync", lambda e: e.dma_start(out=mb_d, in_=m_b),
                                waits=[(sACT, t_rb)], inc=dDBG)
                        S_.emit("sync", lambda e: e.dma_start(out=rb_d, in_=r_b),
                                waits=[(sACT, t_rb)], inc=dDBG)
                        S_.emit("sync", lambda e: e.dma_start(out=lam_dbg, in_=lamneg_b),
                                waits=[(sACT, tok_lam)], inc=dDBG)
                        S_.emit("sync", lambda e: e.dma_start(out=lsc_dbg, in_=lsc),
                                waits=[(sACT, tok_lam)], inc=dDBG)

                    atok = [0] * NT
                    for t in range(NT):
                        w = [(sACT, t_rb)]
                        if t >= 2:
                            w.append((sDVE, atok[t - 2]))
                        xt2 = S_.emit(
                            "sync",
                            lambda e, t=t: e.dma_start(out=xring[t % 2],
                                                       in_=xT[t * 128:(t + 1) * 128, :]),
                            waits=w, inc=dX[t % 2])
                        S_.emit("vector",
                                lambda e, t=t: e.tensor_tensor(out=xring[t % 2],
                                                               in0=xring[t % 2], in1=m_b,
                                                               op=ALU.subtract),
                                waits=[(dX[t % 2], xt2), (sACT, t_rb)], inc=sDVE)
                        atok[t] = S_.emit(
                            "vector",
                            lambda e, t=t: e.tensor_tensor(out=xn[:, t * S:(t + 1) * S],
                                                           in0=xring[t % 2], in1=r_b,
                                                           op=ALU.mult), inc=sDVE)
                    tok_xn = atok[NT - 1]
                    if debug:
                        S_.emit("gpsimd", lambda e: e.dma_start(out=xn_d, in_=xn),
                                waits=[(sDVE, tok_xn)], inc=dDBG)
                        S_.emit("sync", lambda e: e.dma_start(out=mb_d2, in_=m_b),
                                waits=[(sDVE, tok_xn)], inc=dDBG)
                        S_.emit("sync", lambda e: e.dma_start(out=rb_d2, in_=r_b),
                                waits=[(sDVE, tok_xn)], inc=dDBG)
                        S_.emit("sync", lambda e: e.dma_start(out=xr_d[:, 0:S], in_=xring[0]),
                                waits=[(sDVE, tok_xn)], inc=dDBG)
                        S_.emit("sync", lambda e: e.dma_start(out=xr_d[:, S:2 * S], in_=xring[1]),
                                waits=[(sDVE, tok_xn)], inc=dDBG)

            with ExitStack() as p1:
                cos_s = sb(p1, "cos_s", [128, S], F32)
                sin_s = sb(p1, "sin_s", [128, S], F32)
                wv_bf = [sb(p1, "wv0", [128, NT * 256], BF16),
                         sb(p1, "wv1", [128, NT * 256], BF16)]
                wq_bf = [sb(p1, "wq0", [128, NT * 128], BF16),
                         sb(p1, "wq1", [128, NT * 128], BF16)]
                wk_bf = [sb(p1, "wk0", [128, NT * 128], BF16),
                         sb(p1, "wk1", [128, NT * 128], BF16)]
                qs = [sb(p1, "qs0", [128, S], BF16), sb(p1, "qs1", [128, S], BF16)]
                rt0 = sb(p1, "rt0", [128, 512], F32)
                rt1 = sb(p1, "rt1", [128, 512], F32)
                ps_v = [ps(p1, "ps_v0", [128, 256]), ps(p1, "ps_v1", [128, 256])]
                ps_qk = [ps(p1, "ps_qk0", [128, 512]), ps(p1, "ps_qk1", [128, 512])]
                ps_rot = [ps(p1, "ps_rot0", [128, 512]), ps(p1, "ps_rot1", [128, 512])]

                g_p1 = gate()
                S_.emit("sync", lambda e: e.dma_start(out=cos_s, in_=dcos),
                        waits=g_p1, inc=dCONST)
                tok_cs = S_.emit("sync", lambda e: e.dma_start(out=sin_s, in_=dsin),
                                 inc=dCONST)
                wqkv3 = wqkvT.rearrange("(t p) c -> p t c", p=128)
                first_gp1 = [True]

                vmm = {}
                vev = {}
                for dg in range(8):
                    w = [(sMM, vmm[(dg - 2, 15)])] if dg >= 2 else []
                    if first_gp1[0]:
                        w = list(g_p1) + w
                        first_gp1[0] = False
                    wtok = 0
                    for sp in range(4):
                        src = wqkv3[:, sp * 4:(sp + 1) * 4,
                                    2 * E + dg * 256:2 * E + (dg + 1) * 256]
                        dsl = wv_bf[dg % 2].rearrange("p (t c) -> p t c", c=256)[
                            :, sp * 4:(sp + 1) * 4, :]
                        wtok = S_.emit("gpsimd",
                                       lambda e, d=dsl, s=src: e.dma_start(out=d, in_=s),
                                       waits=w, inc=dWV[dg % 2])
                        w = []
                    for ib in range(16):
                        g = dg * 16 + ib
                        def vmmf(e, dg=dg, ib=ib):
                            ins = None
                            for t in range(NT):
                                ins = e.matmul(
                                    ps_v[ib % 2],
                                    xn[:, t * S + ib * 128:t * S + (ib + 1) * 128],
                                    wv_bf[dg % 2][:, t * 256:(t + 1) * 256],
                                    start=(t == 0), stop=(t == NT - 1))
                            return ins
                        w = [(sDVE, tok_xn), (dWV[dg % 2], wtok)]
                        if g < 2:
                            w.append((sACT, tok_lam))
                        if g >= 2:
                            w.append((sACT, vev[g - 2]))
                        vmm[(dg, ib)] = S_.emit("tensor", vmmf, waits=w, inc=sMM)
                        vout = v4[:, ib, 2 * dg:2 * dg + 2, 0:HD]
                        vev[g] = S_.emit(
                            "scalar",
                            lambda e, ib=ib, vo=vout: e.copy(
                                out=vo,
                                in_=ps_v[ib % 2].rearrange("p (h d) -> p h d", d=HD)),
                            waits=[(sMM, vmm[(dg, ib)])], inc=sACT)
                tok_vdone = vev[127]
                if debug:
                    S_.emit("sync", lambda e: e.dma_start(out=vall_d, in_=v_all),
                            waits=[(sACT, tok_vdone), (sDVE, tok_vones)], inc=dDBG)

                qk_ev = {}
                rope_t2 = {}
                rope_done = {}
                w_last_mm = {}
                qk_store = {}
                for h in range(H):
                    for side in range(2):
                        n = 2 * h + side
                        wbuf = wq_bf[h % 2] if side == 0 else wk_bf[h % 2]
                        wsem = dWQ[h % 2] if side == 0 else dWK[h % 2]
                        colbase = h * 128 if side == 0 else E + h * 128
                        w = [(sMM, w_last_mm[n - 4])] if n >= 4 else []
                        wtok = 0
                        for sp in range(4):
                            src = wqkv3[:, sp * 4:(sp + 1) * 4, colbase:colbase + 128]
                            dsl = wbuf.rearrange("p (t c) -> p t c", c=128)[
                                :, sp * 4:(sp + 1) * 4, :]
                            wtok = S_.emit("gpsimd",
                                           lambda e, d=dsl, s=src: e.dma_start(out=d,
                                                                               in_=s),
                                           waits=w, inc=wsem)
                            w = []
                        def emit_rope(g):
                            n2, c = g // 4, g % 4
                            nn = n2 % 2
                            w = [(sACT, qk_ev[g]), (dCONST, tok_cs)]
                            if g >= 2:
                                w.append((sDVE, rope_t2[g - 2]))
                            tokrot = S_.emit(
                                "tensor",
                                lambda e, nn=nn, c=c: e.matmul(
                                    ps_rot[c % 2], PT_s,
                                    qs[nn][:, c * 512:(c + 1) * 512],
                                    start=True, stop=True),
                                waits=w, inc=sMM)
                            S_.emit("vector",
                                    lambda e, nn=nn, c=c: e.tensor_tensor(
                                        out=rt0,
                                        in0=qs[nn][:, c * 512:(c + 1) * 512],
                                        in1=cos_s[:, c * 512:(c + 1) * 512],
                                        op=ALU.mult),
                                    waits=[(sACT, qk_ev[g])], inc=sDVE)
                            rope_t2[g] = S_.emit(
                                "vector",
                                lambda e, c=c: e.tensor_tensor(
                                    out=rt1, in0=ps_rot[c % 2],
                                    in1=sin_s[:, c * 512:(c + 1) * 512], op=ALU.mult),
                                waits=[(sMM, tokrot)], inc=sDVE)
                            rope_done[g] = S_.emit(
                                "vector",
                                lambda e, nn=nn, c=c: e.tensor_tensor(
                                    out=qs[nn][:, c * 512:(c + 1) * 512],
                                    in0=rt0, in1=rt1, op=ALU.add), inc=sDVE)

                        for c in range(4):
                            g = n * 4 + c
                            def qmm(e, wbuf=wbuf, c=c):
                                ins = None
                                for t in range(NT):
                                    ins = e.matmul(ps_qk[c % 2],
                                                   wbuf[:, t * 128:(t + 1) * 128],
                                                   xn[:, t * S + c * 512:t * S + (c + 1) * 512],
                                                   start=(t == 0), stop=(t == NT - 1))
                                return ins
                            w = [(sDVE, tok_xn), (wsem, wtok)]
                            if g >= 2:
                                w.append((sACT, qk_ev[g - 2]))
                            tokmm = S_.emit("tensor", qmm, waits=w, inc=sMM)
                            w = [(sMM, tokmm)]
                            if n >= 2 and c == 0:
                                w.append((dQS[n % 2], qk_store[n - 2]))
                            qk_ev[g] = S_.emit(
                                "scalar",
                                lambda e, n=n, c=c: e.copy(
                                    out=qs[n % 2][:, c * 512:(c + 1) * 512],
                                    in_=ps_qk[c % 2]),
                                waits=w, inc=sACT)
                            if g >= 1:
                                emit_rope(g - 1)
                        if n == 31:
                            emit_rope(127)
                        w_last_mm[n] = S_.cnt[sMM]
                        if n >= 1:
                            qk_store[n - 1] = S_.emit(
                                "sync",
                                lambda e, n=n: e.dma_start(
                                    out=qk_scr[(n - 1) * 128:n * 128, :],
                                    in_=qs[(n - 1) % 2]),
                                waits=[(sDVE, rope_done[(n - 1) * 4 + 3])],
                                inc=dQS[(n - 1) % 2])
                        if n == 31:
                            qk_store[31] = S_.emit(
                                "sync",
                                lambda e: e.dma_start(
                                    out=qk_scr[31 * 128:32 * 128, :], in_=qs[1]),
                                waits=[(sDVE, rope_done[127])], inc=dQS[1])

            with ExitStack() as p2:
                wg_bf = [sb(p2, "wg0", [128, NT * 128], BF16),
                         sb(p2, "wg1", [128, NT * 128], BF16)]
                wu_bf = [sb(p2, "wu0", [128, NT * 128], BF16),
                         sb(p2, "wu1", [128, NT * 128], BF16)]
                silu_t = [sb(p2, "silu0", [128, 512], F32),
                          sb(p2, "silu1", [128, 512], F32)]
                gated_sb = [sb(p2, "gsb0", [128, S], BF16), sb(p2, "gsb1", [128, S], BF16)]
                ps_g = [ps(p2, "ps_g0", [128, 512]), ps(p2, "ps_g1", [128, 512])]
                ps_u = [ps(p2, "ps_u0", [128, 512]), ps(p2, "ps_u1", [128, 512])]
                wg3 = wgT.rearrange("(t p) c -> p t c", p=128)
                wu3 = wuT.rearrange("(t p) c -> p t c", p=128)
                g_p2 = gate() + [(dQS[0], S_.cnt[dQS[0]]), (dQS[1], S_.cnt[dQS[1]])]
                first_gp2 = [True]

                gmm = {}
                gsilu = {}
                gth = {}
                gmult = {}
                gstore = {}
                for ht in range(64):
                    wbase = [(sMM, gmm[(ht - 2) * 4 + 3])] if ht >= 2 else []
                    if first_gp2[0]:
                        wbase = list(g_p2) + wbase
                        first_gp2[0] = False
                    wgtok = 0
                    wutok = 0
                    w = list(wbase)
                    for sp in range(4):
                        src = wg3[:, sp * 4:(sp + 1) * 4, ht * 128:(ht + 1) * 128]
                        dsl = wg_bf[ht % 2].rearrange("p (t c) -> p t c", c=128)[
                            :, sp * 4:(sp + 1) * 4, :]
                        wgtok = S_.emit("gpsimd",
                                        lambda e, d=dsl, s=src: e.dma_start(out=d, in_=s),
                                        waits=w, inc=dWG[ht % 2])
                        w = []
                    w = list(wbase)
                    for sp in range(4):
                        src = wu3[:, sp * 4:(sp + 1) * 4, ht * 128:(ht + 1) * 128]
                        dsl = wu_bf[ht % 2].rearrange("p (t c) -> p t c", c=128)[
                            :, sp * 4:(sp + 1) * 4, :]
                        wutok = S_.emit("gpsimd",
                                        lambda e, d=dsl, s=src: e.dma_start(out=d, in_=s),
                                        waits=w, inc=dWU[ht % 2])
                        w = []
                    for it2 in range(4):
                        g = ht * 4 + it2
                        def gumm(e, ht=ht, it2=it2, g=g):
                            ins = None
                            for t in range(NT):
                                ins = e.matmul(ps_g[g % 2],
                                               wg_bf[ht % 2][:, t * 128:(t + 1) * 128],
                                               xn[:, t * S + it2 * 512:t * S + (it2 + 1) * 512],
                                               start=(t == 0), stop=(t == NT - 1))
                            for t in range(NT):
                                ins = e.matmul(ps_u[g % 2],
                                               wu_bf[ht % 2][:, t * 128:(t + 1) * 128],
                                               xn[:, t * S + it2 * 512:t * S + (it2 + 1) * 512],
                                               start=(t == 0), stop=(t == NT - 1))
                            return ins
                        w = [(sDVE, tok_xn), (dWG[ht % 2], wgtok), (dWU[ht % 2], wutok)]
                        if g >= 2:
                            w += [(sDVE, gth[g - 2]), (sDVE, gmult[g - 2])]
                        gmm[g] = S_.emit("tensor", gumm, waits=w, inc=sMM)
                        w = [(sMM, gmm[g])]
                        if g >= 2:
                            w.append((sDVE, gmult[g - 2]))
                        gsilu[g] = S_.emit(
                            "scalar",
                            lambda e, g=g: e.activation(out=silu_t[g % 2],
                                                        in_=ps_g[g % 2],
                                                        func=AF.Tanh, scale=0.5),
                            waits=w, inc=sACT)
                        gth[g] = S_.emit(
                            "vector",
                            lambda e, g=g: e.scalar_tensor_tensor(
                                out=silu_t[g % 2], in0=silu_t[g % 2], scalar=1.0,
                                in1=ps_g[g % 2], op0=ALU.add, op1=ALU.mult),
                            waits=[(sACT, gsilu[g])], inc=sDVE)
                        w = []
                        if ht >= 2 and it2 == 0:
                            w.append((dGS[ht % 2], gstore[ht - 2]))
                        gmult[g] = S_.emit(
                            "vector",
                            lambda e, it2=it2, g=g: e.scalar_tensor_tensor(
                                out=gated_sb[ht % 2 if False else 0][:, 0:1]
                                if False else
                                gated_sb[(g // 4) % 2][:, it2 * 512:(it2 + 1) * 512],
                                in0=silu_t[g % 2], in1=ps_u[g % 2], scalar=0.5,
                                op0=ALU.mult, op1=ALU.mult),
                            waits=w, inc=sDVE)
                    gstore[ht] = S_.emit(
                        "sync",
                        lambda e, ht=ht: e.dma_start(
                            out=gated_scr[ht * 128:(ht + 1) * 128, :],
                            in_=gated_sb[ht % 2]),
                        waits=[(sDVE, gmult[ht * 4 + 3])], inc=dGS[ht % 2])
                tok_gs = (S_.cnt[dGS[0]], S_.cnt[dGS[1]])
                for ci in range(32):
                    src = (woT.rearrange("(t p) (eo d) -> eo p t d", p=128, d=128)[ci]
                           if ci < 16 else
                           wdT.rearrange("(t p) (eo d) -> eo p t d", p=128, d=128)[ci - 16])
                    dst = wob_scr[ci] if ci < 16 else wdb_scr[ci - 16]
                    S_.emit("gpsimd",
                            lambda e, d2=dst, s2=src: e.dma_start(
                                out=d2.rearrange("p (t d) -> p t d", d=128), in_=s2),
                            inc=dWC)
                tok_wc = S_.cnt[dWC]

        with ExitStack() as p3:
            expb2 = [sb(p3, "expA", [128, 2 * 16 * 256], BF16),
                     sb(p3, "expB", [128, 2 * 16 * 256], BF16)]
            expb = [[expb2[0][:, 0:4096], expb2[1][:, 0:4096]],
                    [expb2[0][:, 4096:8192], expb2[1][:, 4096:8192]]]
            q_sb = [sb(p3, "q0", [128, S], BF16), sb(p3, "q1", [128, S], BF16)]
            k_sb = [sb(p3, "k0", [128, S], BF16), sb(p3, "k1", [128, S], BF16)]
            stage = sb(p3, "stage", [128, S], BF16)
            crow = sb(p3, "crow", [128, 8], F32)
            ct1 = sb(p3, "ct1", [128, 128], F32)
            yts = [sb(p3, "yts0", [128, S], BF16), sb(p3, "yts1", [128, S], BF16)]
            pvcopy = [sb(p3, "pvc0", [128, 2 * VSEC], F32),
                      sb(p3, "pvc1", [128, 2 * VSEC], F32)]
            stQ = [ps(p3, "stQ0", [128, 1024]), ps(p3, "stQ1", [128, 1024]),
                   ps(p3, "stQ2", [128, 1024])]
            pv_ps = ps(p3, "pv_ps", [128, 2 * VSEC])
            tr_ps = ps(p3, "tr_ps", [128, 1024], BF16)

            g_p3 = gate() + [(dGS[0], tok_gs[0]), (dGS[1], tok_gs[1])]
            gst = 0
            gpv = 0
            grp_exp = {}
            exp_br_tok = {}
            pv_tok = {}
            pv_exp_wait = {}
            pv_mask_wait = {}
            comb_tok = {}
            stage_tok = {}
            tr_tok = {}
            pending_act = []
            trev_tok = {}
            ystore = {}
            head_last_mm = {}
            qload = {}
            kload = {}

            for h in range(H):
                w = [(dQS[0], qk_store[2 * h])]
                if h < 2:
                    w += g_p3
                if h >= 2:
                    w.append((sMM, head_last_mm[h - 2]))
                qload[h] = S_.emit(
                    "sync",
                    lambda e, h=h: e.dma_start(
                        out=q_sb[h % 2],
                        in_=qk_scr[2 * h * 128:(2 * h + 1) * 128, :]),
                    waits=w, inc=dQL[h % 2])
                w = [(dQS[1], qk_store[2 * h + 1])]
                if h >= 2:
                    w.append((sMM, head_last_mm[h - 2]))
                kload[h] = S_.emit(
                    "sync",
                    lambda e, h=h: e.dma_start(
                        out=k_sb[h % 2],
                        in_=qk_scr[(2 * h + 1) * 128:(2 * h + 2) * 128, :]),
                    waits=w, inc=dKL[h % 2])

                for it in range(9):
                    itp = it - 1
                    while pending_act:
                        pending_act.pop(0)()

                    def emit_round(g2, it=it):
                        nonlocal gst
                        gG = gst
                        gst += 1
                        jbF = 2 * it + 2
                        buf = stQ[gG % 3]

                        def roundf(e, h=h, it=it, g2=g2, buf=buf):
                            ins = None
                            q0 = q_sb[h % 2][0:64, it * 256:(it + 1) * 256]
                            q1 = q_sb[h % 2][64:128, it * 256:(it + 1) * 256]
                            for jl in range(2):
                                jb = g2 * 2 + jl
                                co = jl * 256
                                kb = jb * 128
                                e.matmul(buf[0:64, co:co + 256],
                                         k_sb[h % 2][0:64, kb:kb + 64], q0,
                                         start=True, stop=True, tile_position=(0, 0))
                                e.matmul(buf[64:128, co:co + 256],
                                         k_sb[h % 2][0:64, kb + 64:kb + 128], q0,
                                         start=True, stop=True, tile_position=(0, 64))
                                e.matmul(buf[0:64, 512 + co:512 + co + 256],
                                         k_sb[h % 2][64:128, kb:kb + 64], q1,
                                         start=True, stop=True, tile_position=(64, 0))
                                ins = e.matmul(buf[64:128, 512 + co:512 + co + 256],
                                               k_sb[h % 2][64:128, kb + 64:kb + 128],
                                               q1, start=True, stop=True,
                                               tile_position=(64, 64))
                            return ins
                        w = [(dQL[h % 2], qload[h]), (dKL[h % 2], kload[h])]
                        if gG < 3:
                            w += g_p3
                        if gG >= 3:
                            w.append((sACT, grp_exp[gG - 3]))
                        tokr = S_.emit("tensor", roundf, waits=w, inc=sMM)
                        bb = bias_n if 2 * g2 < jbF else bias_f
                        w = [(sMM, tokr), (sDVE, tok_bias)]
                        if (h, it - 2) in pv_tok:
                            w.append((sMM, pv_tok[(h, it - 2)]))
                        elif h >= 1 and it < 2:
                            w.append((sMM, pv_tok[(h - 1, it + 6)]))
                        ebuf = expb2[it % 2].rearrange("p (b c) -> p b c", c=4096)
                        sview = buf[:].rearrange("p (b c) -> p b c", c=512)

                        def expf(e, it=it, g2=g2, bb=bb, ebuf=ebuf, sview=sview):
                            return e.activation(
                                out=ebuf[:, :, g2 * 512:(g2 + 1) * 512],
                                in_=sview, func=AF.Exp, scale=SCALER, bias=bb)
                        tok = S_.emit("scalar", expf, waits=w, inc=sACT)
                        for br in range(2):
                            exp_br_tok[(h, it, g2, br)] = tok
                        grp_exp[gG] = tok

                    def emit_pv(is2, part, itp=itp):
                        nonlocal gpv
                        if part == 0:
                            my = gpv
                            gpv += 1
                        else:
                            my = gpv - 1

                        def pvf(e, h=h, itp=itp, is2=is2, part=part):
                            ins = None
                            for jb in range(part * 8, part * 8 + 8):
                                lo = jb * 256 + is2 * 128
                                ins = e.matmul(
                                    pv_ps[:, 0:VSEC],
                                    expb[0][itp % 2][:, lo:lo + 128],
                                    v_all[:, jb * VROW + h * VSEC:
                                          jb * VROW + (h + 1) * VSEC],
                                    start=(jb == 0), stop=False)
                                ins = e.matmul(
                                    pv_ps[:, VSEC:2 * VSEC],
                                    expb[1][itp % 2][:, lo:lo + 128],
                                    v_all[:, jb * VROW + h * VSEC:
                                          jb * VROW + (h + 1) * VSEC],
                                    start=False, stop=(jb == 15))
                            return ins
                        if part == 0:
                            w = [(sACT, pv_exp_wait[(h, itp)]),
                                 (sDVE, pv_mask_wait[(h, itp)]),
                                 (sDVE, tok_vones), (sACT, tok_vdone)]
                            if my >= 1:
                                w.append((sDVE, comb_tok[my - 1]))
                            S_.emit("tensor", pvf, waits=w, inc=sMM)
                            return
                        pv_tok[(h, itp)] = S_.emit("tensor", pvf, inc=sMM)
                        idx = itp * 2 + is2
                        w = [(sMM, pv_tok[(h, itp)]), (sACT, tok_lam)]
                        if h >= 1:
                            w.append((sMM, tr_tok[(h - 1, idx)]))

                        def recipf(e):
                            e.reciprocal(out=crow[:, 0:1], in_=pv_ps[:, HD:HD + 1])
                            return e.reciprocal(
                                out=crow[:, 1:2],
                                in_=pv_ps[:, VSEC + HD:VSEC + HD + 1])
                        tokrc = S_.emit("vector", recipf, waits=w, inc=sDVE)

                        def combf(e, idx=idx):
                            e.tensor_tensor(out=crow[:, 2:3], in0=crow[:, 1:2],
                                            in1=lamneg_b, op=ALU.mult)
                            e.tensor_single_scalar(out=ct1, in_=pv_ps[:, 0:HD],
                                                   scalar=crow[:, 0:1], op=ALU.mult)
                            return e.scalar_tensor_tensor(
                                out=stage[:, idx * 128:(idx + 1) * 128],
                                in0=pv_ps[:, VSEC:VSEC + HD],
                                scalar=crow[:, 2:3], in1=ct1,
                                op0=ALU.mult, op1=ALU.add)
                        comb_tok[my] = S_.emit("vector", combf,
                                               waits=[(sDVE, tokrc)], inc=sDVE)
                        stage_tok[my] = comb_tok[my]

                    def emit_tr(is2, itp=itp):
                        idx = itp * 2 + is2
                        w = [(sDVE, stage_tok[h * 16 + idx])]
                        if idx < 8:
                            if h >= 1:
                                w.append((sACT, trev_tok[(h - 1, 1)]))
                        else:
                            w.append((sACT, trev_tok[(h, 0)]))
                        tr_tok[(h, idx)] = S_.emit(
                            "tensor",
                            lambda e, idx=idx: e.transpose(
                                tr_ps[:, (idx % 8) * 128:(idx % 8 + 1) * 128],
                                stage[:, idx * 128:(idx + 1) * 128], ident_s),
                            waits=w, inc=sMM)
                        if idx == 7:
                            def flushA(h=h):
                                w2 = [(sMM, tr_tok[(h, 7)])]
                                if h >= 2:
                                    w2.append((dYT, ystore[h - 2]))
                                trev_tok[(h, 0)] = S_.emit(
                                    "scalar",
                                    lambda e, h=h: e.copy(out=yts[h % 2][:, 0:1024],
                                                          in_=tr_ps[:]),
                                    waits=w2, inc=sACT)
                            pending_act.append(flushA)
                        elif idx == 15:
                            def flushB(h=h):
                                trev_tok[(h, 1)] = S_.emit(
                                    "scalar",
                                    lambda e, h=h: e.copy(out=yts[h % 2][:, 1024:2048],
                                                          in_=tr_ps[:]),
                                    waits=[(sMM, tr_tok[(h, 15)])], inc=sACT)
                                ystore[h] = S_.emit(
                                    "sync",
                                    lambda e, h=h: e.dma_start(
                                        out=yt_scr[h * 128:(h + 1) * 128, :],
                                        in_=yts[h % 2]),
                                    waits=[(sACT, trev_tok[(h, 1)])], inc=dYT)
                            pending_act.append(flushB)

                    if 1 <= it < 8:
                        emit_round(0)
                        emit_round(1)
                        emit_pv(0, 0)
                        emit_round(2)
                        emit_pv(0, 1)
                        emit_round(3)
                        emit_round(4)
                        emit_pv(1, 0)
                        emit_round(5)
                        emit_pv(1, 1)
                        emit_round(6)
                        emit_tr(0)
                        emit_tr(1)
                        emit_round(7)
                    elif it == 0:
                        for g2 in range(8):
                            emit_round(g2)
                    else:
                        emit_pv(0, 0)
                        emit_pv(0, 1)
                        emit_pv(1, 0)
                        emit_pv(1, 1)
                        emit_tr(0)
                        emit_tr(1)
                    if it < 8:
                        for br in range(2):
                            for dj in range(2):
                                jb = 2 * it + dj
                                cov = exp_br_tok[(h, it, jb // 2, br)]
                                mm_ = m0_s if dj == 0 else m1_s
                                S_.emit(
                                    "vector",
                                    lambda e, br=br, it=it, jb=jb, mm_=mm_:
                                    e.tensor_tensor(
                                        out=expb[br][it % 2][:,
                                                             jb * 256:(jb + 1) * 256],
                                        in0=expb[br][it % 2][:,
                                                             jb * 256:(jb + 1) * 256],
                                        in1=mm_, op=ALU.mult),
                                    waits=[(sACT, cov)], inc=sDVE)
                        pv_exp_wait[(h, it)] = S_.cnt[sACT]
                        pv_mask_wait[(h, it)] = S_.cnt[sDVE]
                head_last_mm[h] = S_.cnt[sMM]
            while pending_act:
                pending_act.pop(0)()
            tok_yt_all = S_.cnt[dYT]

    with ExitStack() as p35p4:
        g_p35 = gate()
        yn = sb(p35p4, "yn", [128, H * S], BF16)
        gam_s = sb(p35p4, "gam_s", [1, S], F32)
        bet_s = sb(p35p4, "bet_s", [1, S], F32)
        tok_gb = None
        gx = list(g_p35)
        for d_, s_2 in ((gam_s, gam_d), (bet_s, bet_d)):
            tok_gb = S_.emit("sync",
                             lambda e, d=d_, s=s_2: e.dma_start(out=d, in_=s),
                             waits=gx, inc=dCONST)
            gx = ()
        for t in range(NT):
            S_.emit(
                "sync",
                lambda e, t=t: e.dma_start(out=yn[:, t * S:(t + 1) * S],
                                           in_=yt_scr[t * 128:(t + 1) * 128, :]),
                waits=[(dYT, tok_yt_all)], inc=dYN)
        tok_ynall = S_.cnt[dYN]
        ysq = [sb(p35p4, "ysq0", [128, S], BF16), sb(p35p4, "ysq1", [128, S], BF16)]
        A_b = sb(p35p4, "A_b", [128, S], BF16)
        B_b = sb(p35p4, "B_b", [128, S], BF16)
        r16 = sb(p35p4, "r16", [1, 64], F32)
        rx = sb(p35p4, "rx", [1, S], F32)
        mx = sb(p35p4, "mx", [1, S], F32)
        arow = sb(p35p4, "arow", [1, S], F32)
        brow = sb(p35p4, "brow", [1, S], F32)

        with ExitStack() as p35a:
            ps_s1 = ps(p35a, "ps_s1", [1, S])
            ps_s2 = ps(p35a, "ps_s2", [1, S])
            sqt = {}
            smt = {}
            for t in range(NT):
                w = [(dYN, tok_ynall)] + list(g_p35)
                if t >= 2:
                    w.append((sMM, smt[t - 2]))
                sqt[t] = S_.emit(
                    "vector",
                    lambda e, t=t: e.tensor_tensor(out=ysq[t % 2],
                                                   in0=yn[:, t * S:(t + 1) * S],
                                                   in1=yn[:, t * S:(t + 1) * S],
                                                   op=ALU.mult),
                    waits=w, inc=sDVE)

                def gmmstats(e, t=t):
                    ins = None
                    for k in range(4):
                        ins = e.matmul(ps_s1[:, k * 512:(k + 1) * 512], onescol_s,
                                       yn[:, t * S + k * 512:t * S + (k + 1) * 512],
                                       start=(t == 0), stop=(t == NT - 1))
                    for k in range(4):
                        ins = e.matmul(ps_s2[:, k * 512:(k + 1) * 512], onescol_s,
                                       ysq[t % 2][:, k * 512:(k + 1) * 512],
                                       start=(t == 0), stop=(t == NT - 1))
                    return ins
                smt[t] = S_.emit("tensor", gmmstats,
                                 waits=[(dYN, tok_ynall), (sDVE, sqt[t])] + list(g_p35),
                                 inc=sMM)
            tok_gstats = smt[NT - 1]
            g0 = S_.emit("vector",
                         lambda e: e.reduce_sum(out=r16[0:1, 0:16].unsqueeze(2),
                                                in_=ps_s1[0:1, :].rearrange(
                                                    "p (s c) -> p s c", c=128),
                                                axis=AX.X),
                         waits=[(sMM, tok_gstats)], inc=sDVE)
            g1 = S_.emit("vector",
                         lambda e: e.reduce_sum(out=r16[0:1, 16:32].unsqueeze(2),
                                                in_=ps_s2[0:1, :].rearrange(
                                                    "p (s c) -> p s c", c=128),
                                                axis=AX.X),
                         waits=[(sDVE, g0)], inc=sDVE)
            NG = float(128 * E)
            g2 = S_.emit("vector",
                         lambda e: e.tensor_scalar_mul(out=r16[0:1, 0:16],
                                                       in0=r16[0:1, 0:16],
                                                       scalar1=1.0 / NG),
                         waits=[(sDVE, g1)], inc=sDVE)
            g3 = S_.emit("vector",
                         lambda e: e.tensor_tensor(out=r16[0:1, 32:48],
                                                   in0=r16[0:1, 0:16],
                                                   in1=r16[0:1, 0:16], op=ALU.mult),
                         waits=[(sDVE, g2)], inc=sDVE)
            g4 = S_.emit("vector",
                         lambda e: e.scalar_tensor_tensor(
                             out=r16[0:1, 16:32], in0=r16[0:1, 16:32],
                             scalar=1.0 / NG, in1=r16[0:1, 32:48],
                             op0=ALU.mult, op1=ALU.subtract),
                         waits=[(sDVE, g3)], inc=sDVE)
            g5a = S_.emit("scalar",
                          lambda e: e.activation(out=r16[0:1, 16:32],
                                                 in_=r16[0:1, 16:32],
                                                 func=AF.Ln, bias=eps_b[0:1, :]),
                          waits=[(sDVE, g4)], inc=sACT)
            tr16 = S_.emit("scalar",
                           lambda e: e.activation(out=r16[0:1, 48:64],
                                                  in_=r16[0:1, 16:32],
                                                  func=AF.Exp, scale=-0.5),
                           waits=[(sACT, g5a)], inc=sACT)
            for b2 in range(16):
                S_.emit("vector",
                        lambda e, b2=b2: e.tensor_single_scalar(
                            out=mx[0:1, b2 * 128:(b2 + 1) * 128],
                            in_=onesrowf_s, scalar=r16[0:1, b2:b2 + 1], op=ALU.mult),
                        waits=[(sACT, tr16)] if b2 == 0 else (), inc=sDVE)
                S_.emit("vector",
                        lambda e, b2=b2: e.tensor_single_scalar(
                            out=rx[0:1, b2 * 128:(b2 + 1) * 128],
                            in_=onesrowf_s, scalar=r16[0:1, 48 + b2:49 + b2],
                            op=ALU.mult), inc=sDVE)
            S_.emit("vector",
                    lambda e: e.scalar_tensor_tensor(out=arow, in0=rx, scalar=0.2,
                                                     in1=gam_s, op0=ALU.mult,
                                                     op1=ALU.mult),
                    waits=[(dCONST, tok_gb)], inc=sDVE)
            S_.emit("vector",
                    lambda e: e.tensor_tensor(out=mx, in0=mx, in1=arow, op=ALU.mult),
                    inc=sDVE)
            tbr = S_.emit("vector",
                          lambda e: e.scalar_tensor_tensor(
                              out=brow, in0=bet_s, scalar=0.2, in1=mx,
                              op0=ALU.mult, op1=ALU.subtract), inc=sDVE)

        with ExitStack() as p35b:
            ps_bc2 = ps(p35b, "ps_bc2", [128, S])

            def bc2(e, src):
                ins = None
                for k in range(4):
                    ins = e.matmul(ps_bc2[:, k * 512:(k + 1) * 512], onesrowf_s,
                                   src[:, k * 512:(k + 1) * 512], start=True, stop=True)
                return ins
            t1 = S_.emit("tensor", lambda e: bc2(e, arow), waits=[(sDVE, tbr)], inc=sMM)
            t2 = S_.emit("scalar", lambda e: e.copy(out=A_b, in_=ps_bc2),
                         waits=[(sMM, t1)], inc=sACT)
            t3 = S_.emit("tensor", lambda e: bc2(e, brow), waits=[(sACT, t2)], inc=sMM)
            tBb = S_.emit("scalar", lambda e: e.copy(out=B_b, in_=ps_bc2),
                          waits=[(sMM, t3)], inc=sACT)
            tapply = None
            for t in range(NT):
                S_.emit("vector",
                        lambda e, t=t: e.tensor_tensor(
                            out=ysq[t % 2], in0=yn[:, t * S:(t + 1) * S],
                            in1=A_b, op=ALU.mult),
                        waits=[(sACT, tBb), (sMM, tok_gstats)], inc=sDVE)
                tapply = S_.emit(
                    "vector",
                    lambda e, t=t: e.tensor_tensor(out=yn[:, t * S:(t + 1) * S],
                                                   in0=ysq[t % 2], in1=B_b,
                                                   op=ALU.add), inc=sDVE)
            tok_yn = tapply
            if debug:
                S_.emit("sync", lambda e: e.dma_start(out=ytn_d, in_=yn),
                        waits=[(sDVE, tok_yn)], inc=dDBG)

        with ExitStack() as p4:
            wo_bf = [sb(p4, "wo0", [128, NT * 128], BF16),
                     sb(p4, "wo1", [128, NT * 128], BF16)]
            attn_sb = [sb(p4, "asb0", [128, S], BF16), sb(p4, "asb1", [128, S], BF16)]
            ps_wo = [ps(p4, "ps_wo0", [128, 512]), ps(p4, "ps_wo1", [128, 512])]
            womm = {}
            woev = {}
            wostore = {}
            for eo in range(16):
                w = [(sMM, womm[(eo - 2) * 4 + 3])] if eo >= 2 else []
                w.append((dWC, tok_wc))
                wtok = S_.emit("gpsimd",
                               lambda e, eo=eo: e.dma_start(out=wo_bf[eo % 2],
                                                            in_=wob_scr[eo]),
                               waits=w, inc=dWO[eo % 2])
                for it in range(4):
                    g = eo * 4 + it
                    def womf(e, eo=eo, it=it, g=g):
                        ins = None
                        for t in range(NT):
                            ins = e.matmul(ps_wo[g % 2],
                                           wo_bf[eo % 2][:, t * 128:(t + 1) * 128],
                                           yn[:, t * S + it * 512:t * S + (it + 1) * 512],
                                           start=(t == 0), stop=(t == NT - 1))
                        return ins
                    w = [(sDVE, tok_yn), (dWO[eo % 2], wtok)]
                    if g >= 2:
                        w.append((sACT, woev[g - 2]))
                    womm[g] = S_.emit("tensor", womf, waits=w, inc=sMM)
                    w = [(sMM, womm[g])]
                    if eo >= 2 and it == 0:
                        w.append((dAS[eo % 2], wostore[eo - 2]))
                    woev[g] = S_.emit(
                        "scalar",
                        lambda e, eo=eo, it=it, g=g: e.copy(
                            out=attn_sb[eo % 2][:, it * 512:(it + 1) * 512],
                            in_=ps_wo[g % 2]),
                        waits=w, inc=sACT)
                wostore[eo] = S_.emit(
                    "sync",
                    lambda e, eo=eo: e.dma_start(
                        out=attn_scr[eo * 128:(eo + 1) * 128, :], in_=attn_sb[eo % 2]),
                    waits=[(sACT, woev[eo * 4 + 3])], inc=dAS[eo % 2])
            tok_attn_stores = dict(wostore)

    with ExitStack() as p5:
        gated5 = sb(p5, "gated5", [128, 64 * 1024], BF16)
        wd_bf = [sb(p5, "wd0", [128, 64 * 128], BF16),
                 sb(p5, "wd1", [128, 64 * 128], BF16)]
        attn_t = [sb(p5, "at%d" % i, [128, 512], BF16) for i in range(4)]
        xt_t = [sb(p5, "xt%d" % i, [128, 512], F32) for i in range(4)]
        o1 = [sb(p5, "o1_0", [128, 512], F32), sb(p5, "o1_1", [128, 512], F32)]
        out_sb = [sb(p5, "ou0", [128, 512], F32), sb(p5, "ou1", [128, 512], F32)]
        ps_d = [ps(p5, "ps_d0", [128, 512]), ps(p5, "ps_d1", [128, 512]),
                ps(p5, "ps_d2", [128, 512]), ps(p5, "ps_d3", [128, 512])]
        wd3 = wdT.rearrange("(t p) c -> p t c", p=128)

        g5 = 0
        g_p5 = gate() + [(dAS[0], S_.cnt[dAS[0]]), (dAS[1], S_.cnt[dAS[1]])]
        first_gp5s = [True]
        first_gp5g = [True]
        dchain = {}
        dadd = {}
        ostore = {}
        for ic in range(2):
            gl_tok = None
            for ht in range(64):
                w = [(dGS[0], tok_gs[0]), (dGS[1], tok_gs[1])]
                if first_gp5s[0]:
                    w = list(g_p5) + w
                    first_gp5s[0] = False
                if ic == 1 and ht == 0:
                    w.append((sMM, dchain[31]))
                gl_tok = S_.emit(
                    "sync",
                    lambda e, ic=ic, ht=ht: e.dma_start(
                        out=gated5[:, ht * 1024:(ht + 1) * 1024],
                        in_=gated_scr[ht * 128:(ht + 1) * 128,
                                      ic * 1024:(ic + 1) * 1024]),
                    waits=w, inc=dG5)
            tok_gload = gl_tok
            for eo in range(16):
                w = [(sMM, dchain[g5 - 3])] if g5 >= 4 else []
                if first_gp5g[0]:
                    w = list(g_p5) + w
                    first_gp5g[0] = False
                w.append((dWC, tok_wc))
                wtok = S_.emit("gpsimd",
                               lambda e, eo=eo: e.dma_start(out=wd_bf[eo % 2],
                                                            in_=wdb_scr[eo]),
                               waits=w, inc=dWD[eo % 2])
                for is5 in range(2):
                    my = g5
                    g5 += 1
                    cols = ic * 1024 + is5 * 512
                    w = [(dAS[eo % 2], tok_attn_stores[eo])]
                    if my >= 4:
                        w.append((sDVE, dadd[my - 4]))
                    S_.emit("gpsimd",
                            lambda e, eo=eo, cols=cols, my=my: e.dma_start(
                                out=attn_t[my % 4],
                                in_=attn_scr[eo * 128:(eo + 1) * 128, cols:cols + 512]),
                            waits=w, inc=dT[my % 4])
                    w2 = [(sDVE, dadd[my - 4])] if my >= 4 else []
                    tl = S_.emit("sync",
                                 lambda e, eo=eo, cols=cols, my=my: e.dma_start(
                                     out=xt_t[my % 4],
                                     in_=xT[eo * 128:(eo + 1) * 128, cols:cols + 512]),
                                 waits=w2, inc=dT[my % 4])

                    def dmf(e, eo=eo, is5=is5, my=my):
                        ins = None
                        for t in range(64):
                            ins = e.matmul(
                                ps_d[my % 4],
                                wd_bf[eo % 2][:, t * 128:(t + 1) * 128],
                                gated5[:,
                                       t * 1024 + is5 * 512:t * 1024 + is5 * 512 + 512],
                                start=(t == 0), stop=(t == 63))
                        return ins
                    w = [(dG5, tok_gload), (dWD[eo % 2], wtok)]
                    if my >= 4:
                        w.append((sDVE, dadd[my - 4]))
                    dchain[my] = S_.emit("tensor", dmf, waits=w, inc=sMM)

                    def addf(e, my=my):
                        e.tensor_tensor(out=o1[my % 2], in0=ps_d[my % 4],
                                        in1=attn_t[my % 4], op=ALU.add)
                        return e.tensor_tensor(out=out_sb[my % 2], in0=o1[my % 2],
                                               in1=xt_t[my % 4], op=ALU.add)
                    w = [(sMM, dchain[my]), (dT[my % 4], tl)]
                    if my >= 2:
                        w.append((dOS[my % 2], ostore[my - 2]))
                    dadd[my] = S_.emit("vector", addf, waits=w, inc=sDVE)
                    ostore[my] = S_.emit(
                        "sync",
                        lambda e, eo=eo, cols=cols, my=my: e.dma_start(
                            out=yout[eo * 128:(eo + 1) * 128, cols:cols + 512],
                            in_=out_sb[my % 2]),
                        waits=[(sDVE, dadd[my])], inc=dOS[my % 2])
        final_waits = [(dOS[0], S_.cnt[dOS[0]]), (dOS[1], S_.cnt[dOS[1]])]
        if debug:
            final_waits.append((dDBG, S_.cnt[dDBG]))
        S_.emit("sync", lambda e: e.nop(), waits=final_waits)

    with nc.Block() as block:
        @block.sync
        def _(eng):
            S_.replay("sync", eng)

        @block.gpsimd
        def _(eng):
            S_.replay("gpsimd", eng)

        @block.tensor
        def _(eng):
            S_.replay("tensor", eng)

        @block.vector
        def _(eng):
            S_.replay("vector", eng)

        @block.scalar
        def _(eng):
            S_.replay("scalar", eng)

    stk.close()
    return nc


def kernel(L0_attn_x, L0_attn_w_qkv, L0_attn_wo, L0_attn_lambda_q1, L0_attn_lambda_q2,
           L0_attn_lambda_k1, L0_attn_lambda_k2, L0_attn_gamma, L0_attn_beta,
           L0_ffn_w_gate, L0_ffn_w_up, L0_ffn_w_down):
    x = np.asarray(L0_attn_x, dtype=np.float32)
    B = x.shape[0]
    xTh = np.ascontiguousarray(x.transpose(0, 2, 1))
    wqkvT = np.ascontiguousarray(np.asarray(L0_attn_w_qkv, np.float32).T)
    woT = np.ascontiguousarray(np.asarray(L0_attn_wo, np.float32).T)
    wgT = np.ascontiguousarray(np.asarray(L0_ffn_w_gate, np.float32).T)
    wuT = np.ascontiguousarray(np.asarray(L0_ffn_w_up, np.float32).T)
    wdT = np.ascontiguousarray(np.asarray(L0_ffn_w_down, np.float32).T)
    gam = np.ascontiguousarray(np.asarray(L0_attn_gamma, np.float32).reshape(1, S))
    bet = np.ascontiguousarray(np.asarray(L0_attn_beta, np.float32).reshape(1, S))
    lq1 = np.ascontiguousarray(np.asarray(L0_attn_lambda_q1, np.float32).reshape(1, HD))
    lq2 = np.ascontiguousarray(np.asarray(L0_attn_lambda_q2, np.float32).reshape(1, HD))
    lk1 = np.ascontiguousarray(np.asarray(L0_attn_lambda_k1, np.float32).reshape(1, HD))
    lk2 = np.ascontiguousarray(np.asarray(L0_attn_lambda_k2, np.float32).reshape(1, HD))

    nc = build_program()
    in_maps = []
    for c in range(B):
        in_maps.append({"xT": xTh[c], "wqkvT": wqkvT, "woT": woT, "wgT": wgT,
                        "wuT": wuT, "wdT": wdT, "gam": gam, "bet": bet,
                        "lq1": lq1, "lq2": lq2, "lk1": lk1, "lk2": lk2})
    res = run_bass_kernel_spmd(nc, in_maps, core_ids=list(range(8)))
    out = np.stack([np.asarray(res.results[c]["y"]) for c in range(B)], axis=0)
    return np.ascontiguousarray(out.transpose(0, 2, 1))


# version 18
# speedup vs baseline: 1.0878x; 1.0878x over previous
import numpy as np
import ml_dtypes
from contextlib import ExitStack
import concourse.bass as bass
import concourse.mybir as mybir
from concourse.bass_utils import run_bass_kernel_spmd

F32 = mybir.dt.float32
BF16 = mybir.dt.bfloat16
AF = mybir.ActivationFunctionType
ALU = mybir.AluOpType
AX = mybir.AxisListType
BF = ml_dtypes.bfloat16

S = 2048
E = 2048
H = 16
HD = 128
DH = 8192
NT = E // 128
SCALER = float(HD) ** -0.5
CSHIFT = 6.0
EPS = 1e-5
ILAM = 0.8
VSEC = HD + 1
VROW = H * VSEC


class Sched:
    def __init__(self, nc):
        self.nc = nc
        self.ops = {k: [] for k in ("sync", "gpsimd", "tensor", "vector", "scalar")}
        self.cnt = {}
        self.sems = {}

    def sem(self, stk, name):
        self.sems[name] = stk.enter_context(self.nc.semaphore(name))
        self.cnt[name] = 0
        return name

    def emit(self, engine, build, waits=(), inc=None):
        tok = None
        if inc is not None:
            self.cnt[inc] += 16 if inc.startswith("d") else 1
            tok = self.cnt[inc]
        self.ops[engine].append((tuple((s, v) for (s, v) in waits if v and v > 0),
                                 build, inc, tok))
        return tok

    def replay(self, engine_name, eng):
        hwm = {}
        for waits, build, inc, tok in self.ops[engine_name]:
            for s, v in waits:
                if hwm.get(s, 0) >= v:
                    continue
                eng.wait_ge(self.sems[s], v)
                hwm[s] = v
            ins = build(eng)
            if inc is not None:
                ins.then_inc(self.sems[inc], 16 if inc.startswith("d") else 1)


def _consts():
    ident = np.eye(128, dtype=BF)
    PT = np.zeros((128, 128), dtype=np.float32)
    for e in range(64):
        PT[e + 64, e] = -1.0
        PT[e, e + 64] = 1.0
    ones_col = np.ones((128, 1), dtype=BF)
    ones_row_f = np.ones((1, 128), dtype=np.float32)
    inv = 1.0 / 10000.0 ** (np.arange(0, HD, 2, dtype=np.float64) / HD)
    ang = np.arange(S, dtype=np.float64)[:, None] * inv[None, :]
    ang = np.concatenate([ang, ang], axis=-1)
    cosT = np.cos(ang).T.astype(np.float32)
    sinT = np.sin(ang).T.astype(np.float32)
    e1 = float(np.e)
    m0 = np.where(np.arange(128)[:, None] > np.arange(256)[None, :], e1, 1.0).astype(BF)
    m1 = np.where(np.arange(128)[:, None] + 128 > np.arange(256)[None, :], e1, 1.0).astype(BF)
    return ident, PT.astype(BF), ones_col, ones_row_f, cosT, sinT, m0, m1


def build_program(debug=False):
    nc = bass.Bass("TRN2", target_bir_lowering=False)

    xT = nc.dram_tensor("xT", [E, S], F32, kind="ExternalInput").ap()
    wqkvT = nc.dram_tensor("wqkvT", [E, 3 * E], F32, kind="ExternalInput").ap()
    woT = nc.dram_tensor("woT", [E, E], F32, kind="ExternalInput").ap()
    wgT = nc.dram_tensor("wgT", [E, DH], F32, kind="ExternalInput").ap()
    wuT = nc.dram_tensor("wuT", [E, DH], F32, kind="ExternalInput").ap()
    wdT = nc.dram_tensor("wdT", [DH, E], F32, kind="ExternalInput").ap()
    gam_d = nc.dram_tensor("gam", [1, S], F32, kind="ExternalInput").ap()
    bet_d = nc.dram_tensor("bet", [1, S], F32, kind="ExternalInput").ap()
    lam_d = {k: nc.dram_tensor(k, [1, HD], F32, kind="ExternalInput").ap()
             for k in ("lq1", "lq2", "lk1", "lk2")}
    yout = nc.dram_tensor("y", [E, S], F32, kind="ExternalOutput").ap()

    scrkind = "ExternalOutput" if debug else "Internal"
    qk_scr = nc.dram_tensor("qk_scr", [2 * H * 128, S], BF16, kind=scrkind).ap()
    gated_scr = nc.dram_tensor("gated_scr", [DH, S], BF16, kind=scrkind).ap()
    attn_scr = nc.dram_tensor("attn_scr", [E, S], BF16, kind=scrkind).ap()
    yt_scr = nc.dram_tensor("yt_scr", [H * 128, S], BF16,
                            kind=("ExternalOutput" if debug else "Internal")).ap()
    wob_scr = nc.dram_tensor("wob_scr", [16, 128, NT * 128], BF16, kind="Internal").ap()
    wdb_scr = nc.dram_tensor("wdb_scr", [16, 128, 64 * 128], BF16, kind="Internal").ap()
    if debug:
        xn_d = nc.dram_tensor("xn_d", [128, NT * S], BF16, kind="ExternalOutput").ap()
        vall_d = nc.dram_tensor("vall_d", [128, 16 * VROW], BF16, kind="ExternalOutput").ap()
        yt_d = nc.dram_tensor("yt_d", [128, H * S], BF16, kind="ExternalOutput").ap()
        ytn_d = nc.dram_tensor("ytn_d", [128, H * S], BF16, kind="ExternalOutput").ap()
        mb_d = nc.dram_tensor("mb_d", [128, S], F32, kind="ExternalOutput").ap()
        rb_d = nc.dram_tensor("rb_d", [128, S], F32, kind="ExternalOutput").ap()
        lam_dbg = nc.dram_tensor("lam_dbg", [128, 1], F32, kind="ExternalOutput").ap()
        lsc_dbg = nc.dram_tensor("lsc_dbg", [1, 8], F32, kind="ExternalOutput").ap()
        xn_d2 = nc.dram_tensor("xn_d2", [128, NT * S], BF16, kind="ExternalOutput").ap()
        mb_d2 = nc.dram_tensor("mb_d2", [128, S], F32, kind="ExternalOutput").ap()
        rb_d2 = nc.dram_tensor("rb_d2", [128, S], F32, kind="ExternalOutput").ap()
        xr_d = nc.dram_tensor("xr_d", [128, 2 * S], F32, kind="ExternalOutput").ap()

    c_id, c_PT, c_oc, c_orf, c_cos, c_sin, c_m0, c_m1 = _consts()
    dident = nc.inline_tensor(c_id, "c_ident").ap()
    dPT = nc.inline_tensor(c_PT, "c_PT").ap()
    doc = nc.inline_tensor(c_oc, "c_onescol").ap()
    dorf = nc.inline_tensor(c_orf, "c_onesrowf").ap()
    dcos = nc.inline_tensor(c_cos, "c_cos").ap()
    dsin = nc.inline_tensor(c_sin, "c_sin").ap()
    dm0 = nc.inline_tensor(c_m0, "c_m0").ap()
    dm1 = nc.inline_tensor(c_m1, "c_m1").ap()

    stk = ExitStack()
    S_ = Sched(nc)
    sem = lambda n: S_.sem(stk, n)
    sMM, sACT, sDVE = sem("sMM"), sem("sACT"), sem("sDVE")
    dCONST = sem("dCONST")
    dX = [sem("dX0"), sem("dX1")]
    dWV = [sem("dWV0"), sem("dWV1")]
    dWQ = [sem("dWQ0"), sem("dWQ1")]
    dWK = [sem("dWK0"), sem("dWK1")]
    dQS = [sem("dQS0"), sem("dQS1")]
    dWG = [sem("dWG0"), sem("dWG1")]
    dWU = [sem("dWU0"), sem("dWU1")]
    dGS = [sem("dGS0"), sem("dGS1")]
    dQL = [sem("dQL0"), sem("dQL1")]
    dKL = [sem("dKL0"), sem("dKL1")]
    dWO = [sem("dWO0"), sem("dWO1")]
    dAS = [sem("dAS0"), sem("dAS1")]
    dG5 = sem("dG5")
    dWD = [sem("dWD0"), sem("dWD1")]
    dT = [sem("dT0"), sem("dT1"), sem("dT2"), sem("dT3")]
    dWC = sem("dWC")
    dYT = sem("dYT")
    dYN = sem("dYN")
    dOS = [sem("dOS0"), sem("dOS1")]
    dDBG = sem("dDBG")

    def gate():
        return [(sMM, S_.cnt[sMM]), (sACT, S_.cnt[sACT]), (sDVE, S_.cnt[sDVE])]

    def sb(scope, name, shape, dt):
        return scope.enter_context(nc.sbuf_tensor(name, shape, dt)).ap()

    def ps(scope, name, shape, dt=F32):
        return scope.enter_context(nc.psum_tensor(name, shape, dt)).ap()

    ident_s = sb(stk, "ident_s", [128, 128], BF16)
    PT_s = sb(stk, "PT_s", [128, 128], BF16)
    onescol_s = sb(stk, "onescol_s", [128, 1], BF16)
    onesrowf_s = sb(stk, "onesrowf_s", [1, 128], F32)
    m0_s = sb(stk, "m0_s", [128, 256], BF16)
    m1_s = sb(stk, "m1_s", [128, 256], BF16)
    eps_b = sb(stk, "eps_b", [1, 1], F32)
    bias_n = sb(stk, "bias_n", [128, 1], F32)
    bias_f = sb(stk, "bias_f", [128, 1], F32)
    lamneg_b = sb(stk, "lamneg_b", [128, 1], F32)

    for dst, src in ((ident_s, dident), (PT_s, dPT), (onescol_s, doc),
                     (onesrowf_s, dorf), (m0_s, dm0), (m1_s, dm1)):
        S_.emit("sync", lambda e, d=dst, s=src: e.dma_start(out=d, in_=s), inc=dCONST)
    tok_consts = S_.cnt[dCONST]
    S_.emit("vector", lambda e: e.memset(eps_b, EPS), inc=sDVE)
    S_.emit("vector", lambda e: e.memset(bias_n, -CSHIFT), inc=sDVE)
    tok_bias = S_.emit("vector", lambda e: e.memset(bias_f, -CSHIFT + 1.0), inc=sDVE)

    with ExitStack() as big:
        v_all = sb(big, "v_all", [128, 16 * VROW], BF16)
        v4 = v_all.rearrange("p (jb h d) -> p jb h d", h=H, d=VSEC)
        tok_vones = S_.emit("vector", lambda e: e.memset(v4[:, :, :, HD:HD + 1], 1.0),
                            inc=sDVE)

        with ExitStack() as xnscope:
            xn = sb(xnscope, "xn", [128, NT * S], BF16)

            with ExitStack() as p0:
                xring = [sb(p0, "xr0", [128, S], F32), sb(p0, "xr1", [128, S], F32)]
                m_row = sb(p0, "m_row", [1, S], F32)
                r_row = sb(p0, "r_row", [1, S], F32)
                trow = sb(p0, "trow", [1, S], F32)

                with ExitStack() as p0a:
                    xbf = sb(p0a, "xbf", [128, S], BF16)
                    xsq = sb(p0a, "xsq", [128, S], BF16)
                    ps_sum = ps(p0a, "ps_sum", [1, S])
                    ps_sq = ps(p0a, "ps_sq", [1, S])

                    ctok = [0] * NT
                    qtok = [0] * NT
                    mtok = [0] * NT
                    xtok = [0] * NT
                    for t in range(NT):
                        w = [(sACT, ctok[t - 2])] if t >= 2 else []
                        xtok[t] = S_.emit(
                            "sync",
                            lambda e, t=t: e.dma_start(out=xring[t % 2],
                                                       in_=xT[t * 128:(t + 1) * 128, :]),
                            waits=w, inc=dX[t % 2])
                        w = [(dX[t % 2], xtok[t])]
                        if t >= 1:
                            w += [(sMM, mtok[t - 1]), (sDVE, qtok[t - 1])]
                        ctok[t] = S_.emit(
                            "scalar", lambda e, t=t: e.copy(out=xbf, in_=xring[t % 2]),
                            waits=w, inc=sACT)
                        w = [(sACT, ctok[t])]
                        if t >= 1:
                            w.append((sMM, mtok[t - 1]))
                        qtok[t] = S_.emit(
                            "vector",
                            lambda e: e.tensor_tensor(out=xsq, in0=xbf, in1=xbf,
                                                      op=ALU.mult),
                            waits=w, inc=sDVE)

                        def mmstats(e, t=t):
                            ins = None
                            for k in range(4):
                                ins = e.matmul(ps_sum[:, k * 512:(k + 1) * 512], onescol_s,
                                               xbf[:, k * 512:(k + 1) * 512],
                                               start=(t == 0), stop=(t == NT - 1))
                            for k in range(4):
                                ins = e.matmul(ps_sq[:, k * 512:(k + 1) * 512], onescol_s,
                                               xsq[:, k * 512:(k + 1) * 512],
                                               start=(t == 0), stop=(t == NT - 1))
                            return ins
                        mtok[t] = S_.emit("tensor", mmstats,
                                          waits=[(dCONST, tok_consts), (sACT, ctok[t]),
                                                 (sDVE, qtok[t])], inc=sMM)
                    tok_stats = mtok[NT - 1]
                    tmr = S_.emit("vector",
                                  lambda e: e.tensor_scalar_mul(out=m_row, in0=ps_sum,
                                                                scalar1=1.0 / E),
                                  waits=[(sMM, tok_stats)], inc=sDVE)
                    S_.emit("vector",
                            lambda e: e.tensor_tensor(out=trow, in0=m_row, in1=m_row,
                                                      op=ALU.mult),
                            waits=[(sDVE, tmr)], inc=sDVE)
                    tv = S_.emit("vector",
                                 lambda e: e.scalar_tensor_tensor(
                                     out=trow, in0=ps_sq, scalar=1.0 / E, in1=trow,
                                     op0=ALU.mult, op1=ALU.subtract),
                                 waits=[(sMM, tok_stats)], inc=sDVE)
                    S_.emit("scalar",
                            lambda e: e.activation(out=trow, in_=trow, func=AF.Ln,
                                                   bias=eps_b[0:1, :]),
                            waits=[(sDVE, tv)], inc=sACT)
                    t_r = S_.emit("scalar",
                                  lambda e: e.activation(out=r_row, in_=trow, func=AF.Exp,
                                                         scale=-0.5), inc=sACT)

                with ExitStack() as p0b:
                    m_b = sb(p0b, "m_b", [128, S], F32)
                    r_b = sb(p0b, "r_b", [128, S], F32)
                    lamrow = {k: sb(p0b, k + "_s", [1, HD], F32)
                              for k in ("lq1", "lq2", "lk1", "lk2")}
                    ltmp = sb(p0b, "ltmp", [1, HD], F32)
                    lsc = sb(p0b, "lsc", [1, 8], F32)
                    ps_bc = ps(p0b, "ps_bc", [128, S])
                    ps_lam = ps(p0b, "ps_lam", [128, 1])
                    gk = gate()
                    for k in ("lq1", "lq2", "lk1", "lk2"):
                        S_.emit("sync",
                                lambda e, d=lamrow[k], s=lam_d[k]: e.dma_start(out=d,
                                                                               in_=s),
                                waits=gk, inc=dCONST)
                        gk = ()
                    tok_lrows = S_.cnt[dCONST]

                    def bc(e, src):
                        ins = None
                        for k in range(4):
                            ins = e.matmul(ps_bc[:, k * 512:(k + 1) * 512], onesrowf_s,
                                           src[:, k * 512:(k + 1) * 512],
                                           start=True, stop=True)
                        return ins
                    t1 = S_.emit("tensor", lambda e: bc(e, m_row),
                                 waits=[(sACT, t_r), (sDVE, tmr)], inc=sMM)
                    t2 = S_.emit("scalar", lambda e: e.copy(out=m_b, in_=ps_bc),
                                 waits=[(sMM, t1)], inc=sACT)
                    t3 = S_.emit("tensor", lambda e: bc(e, r_row),
                                 waits=[(sACT, t2)], inc=sMM)
                    t_rb = S_.emit("scalar", lambda e: e.copy(out=r_b, in_=ps_bc),
                                   waits=[(sMM, t3)], inc=sACT)

                    c0 = S_.emit("vector",
                                 lambda e: e.tensor_tensor(out=ltmp, in0=lamrow["lq1"],
                                                           in1=lamrow["lk1"],
                                                           op=ALU.mult),
                                 waits=[(dCONST, tok_lrows)], inc=sDVE)
                    c1 = S_.emit("vector",
                                 lambda e: e.reduce_sum(out=lsc[:, 0:1], in_=ltmp,
                                                        axis=AX.X),
                                 waits=[(sDVE, c0)], inc=sDVE)
                    c2 = S_.emit("vector",
                                 lambda e: e.tensor_tensor(out=ltmp, in0=lamrow["lq2"],
                                                           in1=lamrow["lk2"],
                                                           op=ALU.mult),
                                 waits=[(sDVE, c1)], inc=sDVE)
                    c3 = S_.emit("vector",
                                 lambda e: e.reduce_sum(out=lsc[:, 1:2], in_=ltmp,
                                                        axis=AX.X),
                                 waits=[(sDVE, c2)], inc=sDVE)
                    c4 = S_.emit("scalar",
                                 lambda e: e.activation(out=lsc[:, 2:4], in_=lsc[:, 0:2],
                                                        func=AF.Exp),
                                 waits=[(sDVE, c3)], inc=sACT)
                    c5 = S_.emit("vector",
                                 lambda e: e.tensor_tensor(out=lsc[:, 4:5],
                                                           in0=lsc[:, 2:3],
                                                           in1=lsc[:, 3:4],
                                                           op=ALU.subtract),
                                 waits=[(sACT, c4)], inc=sDVE)
                    c6 = S_.emit("vector",
                                 lambda e: e.tensor_scalar_add(out=lsc[:, 5:6],
                                                               in0=lsc[:, 4:5],
                                                               scalar1=ILAM),
                                 waits=[(sDVE, c5)], inc=sDVE)
                    c7 = S_.emit("vector",
                                 lambda e: e.tensor_scalar_mul(out=lsc[:, 6:7],
                                                               in0=lsc[:, 5:6],
                                                               scalar1=-1.0),
                                 waits=[(sDVE, c6)], inc=sDVE)
                    c8 = S_.emit("tensor",
                                 lambda e: e.matmul(ps_lam, onesrowf_s, lsc[:, 6:7],
                                                    start=True, stop=True),
                                 waits=[(sDVE, c7)], inc=sMM)
                    tok_lam = S_.emit("scalar", lambda e: e.copy(out=lamneg_b, in_=ps_lam),
                                      waits=[(sMM, c8)], inc=sACT)
                    if debug:
                        S_.emit("sync", lambda e: e.dma_start(out=mb_d, in_=m_b),
                                waits=[(sACT, t_rb)], inc=dDBG)
                        S_.emit("sync", lambda e: e.dma_start(out=rb_d, in_=r_b),
                                waits=[(sACT, t_rb)], inc=dDBG)
                        S_.emit("sync", lambda e: e.dma_start(out=lam_dbg, in_=lamneg_b),
                                waits=[(sACT, tok_lam)], inc=dDBG)
                        S_.emit("sync", lambda e: e.dma_start(out=lsc_dbg, in_=lsc),
                                waits=[(sACT, tok_lam)], inc=dDBG)

                    atok = [0] * NT
                    for t in range(NT):
                        w = [(sACT, t_rb)]
                        if t >= 2:
                            w.append((sDVE, atok[t - 2]))
                        xt2 = S_.emit(
                            "sync",
                            lambda e, t=t: e.dma_start(out=xring[t % 2],
                                                       in_=xT[t * 128:(t + 1) * 128, :]),
                            waits=w, inc=dX[t % 2])
                        S_.emit("vector",
                                lambda e, t=t: e.tensor_tensor(out=xring[t % 2],
                                                               in0=xring[t % 2], in1=m_b,
                                                               op=ALU.subtract),
                                waits=[(dX[t % 2], xt2), (sACT, t_rb)], inc=sDVE)
                        atok[t] = S_.emit(
                            "vector",
                            lambda e, t=t: e.tensor_tensor(out=xn[:, t * S:(t + 1) * S],
                                                           in0=xring[t % 2], in1=r_b,
                                                           op=ALU.mult), inc=sDVE)
                    tok_xn = atok[NT - 1]
                    if debug:
                        S_.emit("gpsimd", lambda e: e.dma_start(out=xn_d, in_=xn),
                                waits=[(sDVE, tok_xn)], inc=dDBG)
                        S_.emit("sync", lambda e: e.dma_start(out=mb_d2, in_=m_b),
                                waits=[(sDVE, tok_xn)], inc=dDBG)
                        S_.emit("sync", lambda e: e.dma_start(out=rb_d2, in_=r_b),
                                waits=[(sDVE, tok_xn)], inc=dDBG)
                        S_.emit("sync", lambda e: e.dma_start(out=xr_d[:, 0:S], in_=xring[0]),
                                waits=[(sDVE, tok_xn)], inc=dDBG)
                        S_.emit("sync", lambda e: e.dma_start(out=xr_d[:, S:2 * S], in_=xring[1]),
                                waits=[(sDVE, tok_xn)], inc=dDBG)

            with ExitStack() as p1:
                cos_s = sb(p1, "cos_s", [128, S], F32)
                sin_s = sb(p1, "sin_s", [128, S], F32)
                wv_bf = [sb(p1, "wv0", [128, NT * 256], BF16),
                         sb(p1, "wv1", [128, NT * 256], BF16)]
                wq_bf = [sb(p1, "wq0", [128, NT * 128], BF16),
                         sb(p1, "wq1", [128, NT * 128], BF16)]
                wk_bf = [sb(p1, "wk0", [128, NT * 128], BF16),
                         sb(p1, "wk1", [128, NT * 128], BF16)]
                qs = [sb(p1, "qs0", [128, S], BF16), sb(p1, "qs1", [128, S], BF16)]
                rt0 = sb(p1, "rt0", [128, 512], F32)
                rt1 = sb(p1, "rt1", [128, 512], F32)
                ps_v = [ps(p1, "ps_v0", [128, 256]), ps(p1, "ps_v1", [128, 256])]
                ps_qk = [ps(p1, "ps_qk0", [128, 512]), ps(p1, "ps_qk1", [128, 512])]
                ps_rot = [ps(p1, "ps_rot0", [128, 512]), ps(p1, "ps_rot1", [128, 512])]

                g_p1 = gate()
                S_.emit("sync", lambda e: e.dma_start(out=cos_s, in_=dcos),
                        waits=g_p1, inc=dCONST)
                tok_cs = S_.emit("sync", lambda e: e.dma_start(out=sin_s, in_=dsin),
                                 inc=dCONST)
                wqkv3 = wqkvT.rearrange("(t p) c -> p t c", p=128)
                first_gp1 = [True]

                vmm = {}
                vev = {}
                for dg in range(8):
                    w = [(sMM, vmm[(dg - 2, 15)])] if dg >= 2 else []
                    if first_gp1[0]:
                        w = list(g_p1) + w
                        first_gp1[0] = False
                    wtok = 0
                    for sp in range(4):
                        src = wqkv3[:, sp * 4:(sp + 1) * 4,
                                    2 * E + dg * 256:2 * E + (dg + 1) * 256]
                        dsl = wv_bf[dg % 2].rearrange("p (t c) -> p t c", c=256)[
                            :, sp * 4:(sp + 1) * 4, :]
                        wtok = S_.emit("gpsimd",
                                       lambda e, d=dsl, s=src: e.dma_start(out=d, in_=s),
                                       waits=w, inc=dWV[dg % 2])
                        w = []
                    for ib in range(16):
                        g = dg * 16 + ib
                        def vmmf(e, dg=dg, ib=ib):
                            ins = None
                            for t in range(NT):
                                ins = e.matmul(
                                    ps_v[ib % 2],
                                    xn[:, t * S + ib * 128:t * S + (ib + 1) * 128],
                                    wv_bf[dg % 2][:, t * 256:(t + 1) * 256],
                                    start=(t == 0), stop=(t == NT - 1))
                            return ins
                        w = [(sDVE, tok_xn), (dWV[dg % 2], wtok)]
                        if g < 2:
                            w.append((sACT, tok_lam))
                        if g >= 2:
                            w.append((sACT, vev[g - 2]))
                        vmm[(dg, ib)] = S_.emit("tensor", vmmf, waits=w, inc=sMM)
                        vout = v4[:, ib, 2 * dg:2 * dg + 2, 0:HD]
                        vev[g] = S_.emit(
                            "scalar",
                            lambda e, ib=ib, vo=vout: e.copy(
                                out=vo,
                                in_=ps_v[ib % 2].rearrange("p (h d) -> p h d", d=HD)),
                            waits=[(sMM, vmm[(dg, ib)])], inc=sACT)
                tok_vdone = vev[127]
                if debug:
                    S_.emit("sync", lambda e: e.dma_start(out=vall_d, in_=v_all),
                            waits=[(sACT, tok_vdone), (sDVE, tok_vones)], inc=dDBG)

                qk_ev = {}
                rope_t2 = {}
                rope_done = {}
                w_last_mm = {}
                qk_store = {}
                for h in range(H):
                    for side in range(2):
                        n = 2 * h + side
                        wbuf = wq_bf[h % 2] if side == 0 else wk_bf[h % 2]
                        wsem = dWQ[h % 2] if side == 0 else dWK[h % 2]
                        colbase = h * 128 if side == 0 else E + h * 128
                        w = [(sMM, w_last_mm[n - 4])] if n >= 4 else []
                        wtok = 0
                        for sp in range(4):
                            src = wqkv3[:, sp * 4:(sp + 1) * 4, colbase:colbase + 128]
                            dsl = wbuf.rearrange("p (t c) -> p t c", c=128)[
                                :, sp * 4:(sp + 1) * 4, :]
                            wtok = S_.emit("gpsimd",
                                           lambda e, d=dsl, s=src: e.dma_start(out=d,
                                                                               in_=s),
                                           waits=w, inc=wsem)
                            w = []
                        def emit_rope(g):
                            n2, c = g // 4, g % 4
                            nn = n2 % 2
                            w = [(sACT, qk_ev[g]), (dCONST, tok_cs)]
                            if g >= 2:
                                w.append((sDVE, rope_t2[g - 2]))
                            tokrot = S_.emit(
                                "tensor",
                                lambda e, nn=nn, c=c: e.matmul(
                                    ps_rot[c % 2], PT_s,
                                    qs[nn][:, c * 512:(c + 1) * 512],
                                    start=True, stop=True),
                                waits=w, inc=sMM)
                            S_.emit("vector",
                                    lambda e, nn=nn, c=c: e.tensor_tensor(
                                        out=rt0,
                                        in0=qs[nn][:, c * 512:(c + 1) * 512],
                                        in1=cos_s[:, c * 512:(c + 1) * 512],
                                        op=ALU.mult),
                                    waits=[(sACT, qk_ev[g])], inc=sDVE)
                            rope_t2[g] = S_.emit(
                                "vector",
                                lambda e, c=c: e.tensor_tensor(
                                    out=rt1, in0=ps_rot[c % 2],
                                    in1=sin_s[:, c * 512:(c + 1) * 512], op=ALU.mult),
                                waits=[(sMM, tokrot)], inc=sDVE)
                            rope_done[g] = S_.emit(
                                "vector",
                                lambda e, nn=nn, c=c: e.tensor_tensor(
                                    out=qs[nn][:, c * 512:(c + 1) * 512],
                                    in0=rt0, in1=rt1, op=ALU.add), inc=sDVE)

                        for c in range(4):
                            g = n * 4 + c
                            def qmm(e, wbuf=wbuf, c=c):
                                ins = None
                                for t in range(NT):
                                    ins = e.matmul(ps_qk[c % 2],
                                                   wbuf[:, t * 128:(t + 1) * 128],
                                                   xn[:, t * S + c * 512:t * S + (c + 1) * 512],
                                                   start=(t == 0), stop=(t == NT - 1))
                                return ins
                            w = [(sDVE, tok_xn), (wsem, wtok)]
                            if g >= 2:
                                w.append((sACT, qk_ev[g - 2]))
                            tokmm = S_.emit("tensor", qmm, waits=w, inc=sMM)
                            w = [(sMM, tokmm)]
                            if n >= 2 and c == 0:
                                w.append((dQS[n % 2], qk_store[n - 2]))
                            qk_ev[g] = S_.emit(
                                "scalar",
                                lambda e, n=n, c=c: e.copy(
                                    out=qs[n % 2][:, c * 512:(c + 1) * 512],
                                    in_=ps_qk[c % 2]),
                                waits=w, inc=sACT)
                            if g >= 1:
                                emit_rope(g - 1)
                        if n == 31:
                            emit_rope(127)
                        w_last_mm[n] = S_.cnt[sMM]
                        if n >= 1:
                            qk_store[n - 1] = S_.emit(
                                "sync",
                                lambda e, n=n: e.dma_start(
                                    out=qk_scr[(n - 1) * 128:n * 128, :],
                                    in_=qs[(n - 1) % 2]),
                                waits=[(sDVE, rope_done[(n - 1) * 4 + 3])],
                                inc=dQS[(n - 1) % 2])
                        if n == 31:
                            qk_store[31] = S_.emit(
                                "sync",
                                lambda e: e.dma_start(
                                    out=qk_scr[31 * 128:32 * 128, :], in_=qs[1]),
                                waits=[(sDVE, rope_done[127])], inc=dQS[1])

            with ExitStack() as p2:
                wg_bf = [sb(p2, "wg0", [128, NT * 128], BF16),
                         sb(p2, "wg1", [128, NT * 128], BF16)]
                wu_bf = [sb(p2, "wu0", [128, NT * 128], BF16),
                         sb(p2, "wu1", [128, NT * 128], BF16)]
                silu_t = [sb(p2, "silu0", [128, 512], F32),
                          sb(p2, "silu1", [128, 512], F32)]
                gated_sb = [sb(p2, "gsb0", [128, S], BF16), sb(p2, "gsb1", [128, S], BF16)]
                ps_g = [ps(p2, "ps_g0", [128, 512]), ps(p2, "ps_g1", [128, 512])]
                ps_u = [ps(p2, "ps_u0", [128, 512]), ps(p2, "ps_u1", [128, 512])]
                wg3 = wgT.rearrange("(t p) c -> p t c", p=128)
                wu3 = wuT.rearrange("(t p) c -> p t c", p=128)
                g_p2 = gate() + [(dQS[0], S_.cnt[dQS[0]]), (dQS[1], S_.cnt[dQS[1]])]
                first_gp2 = [True]

                gmm = {}
                gsilu = {}
                gth = {}
                gmult = {}
                gstore = {}
                for ht in range(64):
                    wbase = [(sMM, gmm[(ht - 2) * 4 + 3])] if ht >= 2 else []
                    if first_gp2[0]:
                        wbase = list(g_p2) + wbase
                        first_gp2[0] = False
                    wgtok = 0
                    wutok = 0
                    w = list(wbase)
                    for sp in range(4):
                        src = wg3[:, sp * 4:(sp + 1) * 4, ht * 128:(ht + 1) * 128]
                        dsl = wg_bf[ht % 2].rearrange("p (t c) -> p t c", c=128)[
                            :, sp * 4:(sp + 1) * 4, :]
                        wgtok = S_.emit("gpsimd",
                                        lambda e, d=dsl, s=src: e.dma_start(out=d, in_=s),
                                        waits=w, inc=dWG[ht % 2])
                        w = []
                    w = list(wbase)
                    for sp in range(4):
                        src = wu3[:, sp * 4:(sp + 1) * 4, ht * 128:(ht + 1) * 128]
                        dsl = wu_bf[ht % 2].rearrange("p (t c) -> p t c", c=128)[
                            :, sp * 4:(sp + 1) * 4, :]
                        wutok = S_.emit("gpsimd",
                                        lambda e, d=dsl, s=src: e.dma_start(out=d, in_=s),
                                        waits=w, inc=dWU[ht % 2])
                        w = []
                    for it2 in range(4):
                        g = ht * 4 + it2
                        def gumm(e, ht=ht, it2=it2, g=g):
                            ins = None
                            for t in range(NT):
                                ins = e.matmul(ps_g[g % 2],
                                               wg_bf[ht % 2][:, t * 128:(t + 1) * 128],
                                               xn[:, t * S + it2 * 512:t * S + (it2 + 1) * 512],
                                               start=(t == 0), stop=(t == NT - 1))
                            for t in range(NT):
                                ins = e.matmul(ps_u[g % 2],
                                               wu_bf[ht % 2][:, t * 128:(t + 1) * 128],
                                               xn[:, t * S + it2 * 512:t * S + (it2 + 1) * 512],
                                               start=(t == 0), stop=(t == NT - 1))
                            return ins
                        w = [(sDVE, tok_xn), (dWG[ht % 2], wgtok), (dWU[ht % 2], wutok)]
                        if g >= 2:
                            w += [(sDVE, gth[g - 2]), (sDVE, gmult[g - 2])]
                        gmm[g] = S_.emit("tensor", gumm, waits=w, inc=sMM)
                        w = [(sMM, gmm[g])]
                        if g >= 2:
                            w.append((sDVE, gmult[g - 2]))
                        gsilu[g] = S_.emit(
                            "scalar",
                            lambda e, g=g: e.activation(out=silu_t[g % 2],
                                                        in_=ps_g[g % 2],
                                                        func=AF.Tanh, scale=0.5),
                            waits=w, inc=sACT)
                        gth[g] = S_.emit(
                            "vector",
                            lambda e, g=g: e.scalar_tensor_tensor(
                                out=silu_t[g % 2], in0=silu_t[g % 2], scalar=1.0,
                                in1=ps_g[g % 2], op0=ALU.add, op1=ALU.mult),
                            waits=[(sACT, gsilu[g])], inc=sDVE)
                        w = []
                        if ht >= 2 and it2 == 0:
                            w.append((dGS[ht % 2], gstore[ht - 2]))
                        gmult[g] = S_.emit(
                            "vector",
                            lambda e, it2=it2, g=g: e.scalar_tensor_tensor(
                                out=gated_sb[ht % 2 if False else 0][:, 0:1]
                                if False else
                                gated_sb[(g // 4) % 2][:, it2 * 512:(it2 + 1) * 512],
                                in0=silu_t[g % 2], in1=ps_u[g % 2], scalar=0.5,
                                op0=ALU.mult, op1=ALU.mult),
                            waits=w, inc=sDVE)
                    gstore[ht] = S_.emit(
                        "sync",
                        lambda e, ht=ht: e.dma_start(
                            out=gated_scr[ht * 128:(ht + 1) * 128, :],
                            in_=gated_sb[ht % 2]),
                        waits=[(sDVE, gmult[ht * 4 + 3])], inc=dGS[ht % 2])
                tok_gs = (S_.cnt[dGS[0]], S_.cnt[dGS[1]])
                for ci in range(32):
                    src = (woT.rearrange("(t p) (eo d) -> eo p t d", p=128, d=128)[ci]
                           if ci < 16 else
                           wdT.rearrange("(t p) (eo d) -> eo p t d", p=128, d=128)[ci - 16])
                    dst = wob_scr[ci] if ci < 16 else wdb_scr[ci - 16]
                    S_.emit("gpsimd",
                            lambda e, d2=dst, s2=src: e.dma_start(
                                out=d2.rearrange("p (t d) -> p t d", d=128), in_=s2),
                            inc=dWC)
                tok_wc = S_.cnt[dWC]

        with ExitStack() as p3:
            expb2 = [sb(p3, "expA", [128, 2 * 16 * 256], BF16),
                     sb(p3, "expB", [128, 2 * 16 * 256], BF16)]
            expb = [[expb2[0][:, 0:4096], expb2[1][:, 0:4096]],
                    [expb2[0][:, 4096:8192], expb2[1][:, 4096:8192]]]
            q_sb = [sb(p3, "q0", [128, S], BF16), sb(p3, "q1", [128, S], BF16)]
            k_sb = [sb(p3, "k0", [128, S], BF16), sb(p3, "k1", [128, S], BF16)]
            stage = sb(p3, "stage", [128, S], BF16)
            crow = sb(p3, "crow", [128, 8], F32)
            ct1 = sb(p3, "ct1", [128, 128], F32)
            yts = [sb(p3, "yts0", [128, S], BF16), sb(p3, "yts1", [128, S], BF16)]
            pvcopy = [sb(p3, "pvc0", [128, 2 * VSEC], F32),
                      sb(p3, "pvc1", [128, 2 * VSEC], F32)]
            stQ = [ps(p3, "stQ0", [128, 1024]), ps(p3, "stQ1", [128, 1024]),
                   ps(p3, "stQ2", [128, 1024])]
            pv_ps = ps(p3, "pv_ps", [128, 2 * VSEC])
            tr_ps = ps(p3, "tr_ps", [128, 1024], BF16)

            g_p3 = gate() + [(dGS[0], tok_gs[0]), (dGS[1], tok_gs[1])]
            gst = 0
            gpv = 0
            grp_exp = {}
            exp_br_tok = {}
            pv_tok = {}
            pv_exp_wait = {}
            pv_mask_wait = {}
            comb_tok = {}
            stage_tok = {}
            tr_tok = {}
            pending_act = []
            trev_tok = {}
            ystore = {}
            head_last_mm = {}
            qload = {}
            kload = {}

            for h in range(H):
                w = [(dQS[0], qk_store[2 * h])]
                if h < 2:
                    w += g_p3
                if h >= 2:
                    w.append((sMM, head_last_mm[h - 2]))
                qload[h] = S_.emit(
                    "sync",
                    lambda e, h=h: e.dma_start(
                        out=q_sb[h % 2],
                        in_=qk_scr[2 * h * 128:(2 * h + 1) * 128, :]),
                    waits=w, inc=dQL[h % 2])
                w = [(dQS[1], qk_store[2 * h + 1])]
                if h >= 2:
                    w.append((sMM, head_last_mm[h - 2]))
                kload[h] = S_.emit(
                    "sync",
                    lambda e, h=h: e.dma_start(
                        out=k_sb[h % 2],
                        in_=qk_scr[(2 * h + 1) * 128:(2 * h + 2) * 128, :]),
                    waits=w, inc=dKL[h % 2])

                for it in range(9):
                    itp = it - 1
                    while pending_act:
                        pending_act.pop(0)()

                    def emit_round(g2, it=it):
                        nonlocal gst
                        gG = gst
                        gst += 1
                        jbF = 2 * it + 2
                        buf = stQ[gG % 3]

                        def roundf(e, h=h, it=it, g2=g2, buf=buf):
                            ins = None
                            q0 = q_sb[h % 2][0:64, it * 256:(it + 1) * 256]
                            q1 = q_sb[h % 2][64:128, it * 256:(it + 1) * 256]
                            for jl in range(2):
                                jb = g2 * 2 + jl
                                co = jl * 256
                                kb = jb * 128
                                e.matmul(buf[0:64, co:co + 256],
                                         k_sb[h % 2][0:64, kb:kb + 64], q0,
                                         start=True, stop=True, tile_position=(0, 0))
                                e.matmul(buf[64:128, co:co + 256],
                                         k_sb[h % 2][0:64, kb + 64:kb + 128], q0,
                                         start=True, stop=True, tile_position=(0, 64))
                                e.matmul(buf[0:64, 512 + co:512 + co + 256],
                                         k_sb[h % 2][64:128, kb:kb + 64], q1,
                                         start=True, stop=True, tile_position=(64, 0))
                                ins = e.matmul(buf[64:128, 512 + co:512 + co + 256],
                                               k_sb[h % 2][64:128, kb + 64:kb + 128],
                                               q1, start=True, stop=True,
                                               tile_position=(64, 64))
                            return ins
                        w = [(dQL[h % 2], qload[h]), (dKL[h % 2], kload[h])]
                        if gG < 3:
                            w += g_p3
                        if gG >= 3:
                            w.append((sACT, grp_exp[gG - 3]))
                        tokr = S_.emit("tensor", roundf, waits=w, inc=sMM)
                        bb = bias_n if 2 * g2 < jbF else bias_f
                        w = [(sMM, tokr), (sDVE, tok_bias)]
                        if (h, it - 2) in pv_tok:
                            w.append((sMM, pv_tok[(h, it - 2)]))
                        elif h >= 1 and it < 2:
                            w.append((sMM, pv_tok[(h - 1, it + 6)]))
                        ebuf = expb2[it % 2].rearrange("p (b c) -> p b c", c=4096)
                        sview = buf[:].rearrange("p (b c) -> p b c", c=512)

                        def expf(e, it=it, g2=g2, bb=bb, ebuf=ebuf, sview=sview):
                            return e.activation(
                                out=ebuf[:, :, g2 * 512:(g2 + 1) * 512],
                                in_=sview, func=AF.Exp, scale=SCALER, bias=bb)
                        tok = S_.emit("scalar", expf, waits=w, inc=sACT)
                        for br in range(2):
                            exp_br_tok[(h, it, g2, br)] = tok
                        grp_exp[gG] = tok

                    def emit_pv(is2, part, itp=itp):
                        nonlocal gpv
                        if part == 0:
                            my = gpv
                            gpv += 1
                        else:
                            my = gpv - 1

                        def pvf(e, h=h, itp=itp, is2=is2, part=part):
                            ins = None
                            for jb in range(part * 4, part * 4 + 4):
                                lo = jb * 256 + is2 * 128
                                ins = e.matmul(
                                    pv_ps[:, 0:VSEC],
                                    expb[0][itp % 2][:, lo:lo + 128],
                                    v_all[:, jb * VROW + h * VSEC:
                                          jb * VROW + (h + 1) * VSEC],
                                    start=(jb == 0), stop=False)
                                ins = e.matmul(
                                    pv_ps[:, VSEC:2 * VSEC],
                                    expb[1][itp % 2][:, lo:lo + 128],
                                    v_all[:, jb * VROW + h * VSEC:
                                          jb * VROW + (h + 1) * VSEC],
                                    start=False, stop=(jb == 15))
                            return ins
                        w = [(sACT, exp_br_tok[(h, itp, 2 * part + 1, 0)])]
                        if part == itp // 2:
                            w.append((sDVE, pv_mask_wait[(h, itp)]))
                        if part == 0:
                            w += [(sDVE, tok_vones), (sACT, tok_vdone)]
                            if my >= 1:
                                w.append((sDVE, comb_tok[my - 1]))
                            S_.emit("tensor", pvf, waits=w, inc=sMM)
                            return
                        if part < 3:
                            S_.emit("tensor", pvf, waits=w, inc=sMM)
                            return
                        pv_tok[(h, itp)] = S_.emit("tensor", pvf, waits=w, inc=sMM)
                        idx = itp * 2 + is2
                        w = [(sMM, pv_tok[(h, itp)]), (sACT, tok_lam)]
                        if h >= 1:
                            w.append((sMM, tr_tok[(h - 1, idx)]))

                        def recipf(e):
                            e.reciprocal(out=crow[:, 0:1], in_=pv_ps[:, HD:HD + 1])
                            return e.reciprocal(
                                out=crow[:, 1:2],
                                in_=pv_ps[:, VSEC + HD:VSEC + HD + 1])
                        tokrc = S_.emit("vector", recipf, waits=w, inc=sDVE)

                        def combf(e, idx=idx):
                            e.tensor_tensor(out=crow[:, 2:3], in0=crow[:, 1:2],
                                            in1=lamneg_b, op=ALU.mult)
                            e.tensor_single_scalar(out=ct1, in_=pv_ps[:, 0:HD],
                                                   scalar=crow[:, 0:1], op=ALU.mult)
                            return e.scalar_tensor_tensor(
                                out=stage[:, idx * 128:(idx + 1) * 128],
                                in0=pv_ps[:, VSEC:VSEC + HD],
                                scalar=crow[:, 2:3], in1=ct1,
                                op0=ALU.mult, op1=ALU.add)
                        comb_tok[my] = S_.emit("vector", combf,
                                               waits=[(sDVE, tokrc)], inc=sDVE)
                        stage_tok[my] = comb_tok[my]

                    def emit_tr(is2, itp=itp):
                        idx = itp * 2 + is2
                        w = [(sDVE, stage_tok[h * 16 + idx])]
                        if idx < 8:
                            if h >= 1:
                                w.append((sACT, trev_tok[(h - 1, 1)]))
                        else:
                            w.append((sACT, trev_tok[(h, 0)]))
                        tr_tok[(h, idx)] = S_.emit(
                            "tensor",
                            lambda e, idx=idx: e.transpose(
                                tr_ps[:, (idx % 8) * 128:(idx % 8 + 1) * 128],
                                stage[:, idx * 128:(idx + 1) * 128], ident_s),
                            waits=w, inc=sMM)
                        if idx == 7:
                            def flushA(h=h):
                                w2 = [(sMM, tr_tok[(h, 7)])]
                                if h >= 2:
                                    w2.append((dYT, ystore[h - 2]))
                                trev_tok[(h, 0)] = S_.emit(
                                    "scalar",
                                    lambda e, h=h: e.copy(out=yts[h % 2][:, 0:1024],
                                                          in_=tr_ps[:]),
                                    waits=w2, inc=sACT)
                            pending_act.append(flushA)
                        elif idx == 15:
                            def flushB(h=h):
                                trev_tok[(h, 1)] = S_.emit(
                                    "scalar",
                                    lambda e, h=h: e.copy(out=yts[h % 2][:, 1024:2048],
                                                          in_=tr_ps[:]),
                                    waits=[(sMM, tr_tok[(h, 15)])], inc=sACT)
                                ystore[h] = S_.emit(
                                    "sync",
                                    lambda e, h=h: e.dma_start(
                                        out=yt_scr[h * 128:(h + 1) * 128, :],
                                        in_=yts[h % 2]),
                                    waits=[(sACT, trev_tok[(h, 1)])], inc=dYT)
                            pending_act.append(flushB)

                    if 1 <= it < 8:
                        emit_round(0)
                        emit_pv(0, 0)
                        emit_round(1)
                        emit_pv(0, 1)
                        emit_round(2)
                        emit_pv(0, 2)
                        emit_round(3)
                        emit_pv(0, 3)
                        emit_round(4)
                        emit_round(5)
                        emit_pv(1, 0)
                        emit_pv(1, 1)
                        emit_round(6)
                        emit_pv(1, 2)
                        emit_pv(1, 3)
                        emit_round(7)
                        emit_tr(0)
                        emit_tr(1)
                    elif it == 0:
                        for g2 in range(8):
                            emit_round(g2)
                    else:
                        for pp in range(4):
                            emit_pv(0, pp)
                        for pp in range(4):
                            emit_pv(1, pp)
                        emit_tr(0)
                        emit_tr(1)
                    if it < 8:
                        for br in range(2):
                            for dj in range(2):
                                jb = 2 * it + dj
                                cov = exp_br_tok[(h, it, jb // 2, br)]
                                mm_ = m0_s if dj == 0 else m1_s
                                S_.emit(
                                    "vector",
                                    lambda e, br=br, it=it, jb=jb, mm_=mm_:
                                    e.tensor_tensor(
                                        out=expb[br][it % 2][:,
                                                             jb * 256:(jb + 1) * 256],
                                        in0=expb[br][it % 2][:,
                                                             jb * 256:(jb + 1) * 256],
                                        in1=mm_, op=ALU.mult),
                                    waits=[(sACT, cov)], inc=sDVE)
                        pv_exp_wait[(h, it)] = S_.cnt[sACT]
                        pv_mask_wait[(h, it)] = S_.cnt[sDVE]
                head_last_mm[h] = S_.cnt[sMM]
            while pending_act:
                pending_act.pop(0)()
            tok_yt_all = S_.cnt[dYT]

    with ExitStack() as p35p4:
        g_p35 = gate()
        yn = sb(p35p4, "yn", [128, H * S], BF16)
        gam_s = sb(p35p4, "gam_s", [1, S], F32)
        bet_s = sb(p35p4, "bet_s", [1, S], F32)
        tok_gb = None
        gx = list(g_p35)
        for d_, s_2 in ((gam_s, gam_d), (bet_s, bet_d)):
            tok_gb = S_.emit("sync",
                             lambda e, d=d_, s=s_2: e.dma_start(out=d, in_=s),
                             waits=gx, inc=dCONST)
            gx = ()
        for t in range(NT):
            S_.emit(
                "sync",
                lambda e, t=t: e.dma_start(out=yn[:, t * S:(t + 1) * S],
                                           in_=yt_scr[t * 128:(t + 1) * 128, :]),
                waits=[(dYT, tok_yt_all)], inc=dYN)
        tok_ynall = S_.cnt[dYN]
        ysq = [sb(p35p4, "ysq0", [128, S], BF16), sb(p35p4, "ysq1", [128, S], BF16)]
        A_b = sb(p35p4, "A_b", [128, S], BF16)
        B_b = sb(p35p4, "B_b", [128, S], BF16)
        r16 = sb(p35p4, "r16", [1, 64], F32)
        rx = sb(p35p4, "rx", [1, S], F32)
        mx = sb(p35p4, "mx", [1, S], F32)
        arow = sb(p35p4, "arow", [1, S], F32)
        brow = sb(p35p4, "brow", [1, S], F32)

        with ExitStack() as p35a:
            ps_s1 = ps(p35a, "ps_s1", [1, S])
            ps_s2 = ps(p35a, "ps_s2", [1, S])
            sqt = {}
            smt = {}
            for t in range(NT):
                w = [(dYN, tok_ynall)] + list(g_p35)
                if t >= 2:
                    w.append((sMM, smt[t - 2]))
                sqt[t] = S_.emit(
                    "vector",
                    lambda e, t=t: e.tensor_tensor(out=ysq[t % 2],
                                                   in0=yn[:, t * S:(t + 1) * S],
                                                   in1=yn[:, t * S:(t + 1) * S],
                                                   op=ALU.mult),
                    waits=w, inc=sDVE)

                def gmmstats(e, t=t):
                    ins = None
                    for k in range(4):
                        ins = e.matmul(ps_s1[:, k * 512:(k + 1) * 512], onescol_s,
                                       yn[:, t * S + k * 512:t * S + (k + 1) * 512],
                                       start=(t == 0), stop=(t == NT - 1))
                    for k in range(4):
                        ins = e.matmul(ps_s2[:, k * 512:(k + 1) * 512], onescol_s,
                                       ysq[t % 2][:, k * 512:(k + 1) * 512],
                                       start=(t == 0), stop=(t == NT - 1))
                    return ins
                smt[t] = S_.emit("tensor", gmmstats,
                                 waits=[(dYN, tok_ynall), (sDVE, sqt[t])] + list(g_p35),
                                 inc=sMM)
            tok_gstats = smt[NT - 1]
            g0 = S_.emit("vector",
                         lambda e: e.reduce_sum(out=r16[0:1, 0:16].unsqueeze(2),
                                                in_=ps_s1[0:1, :].rearrange(
                                                    "p (s c) -> p s c", c=128),
                                                axis=AX.X),
                         waits=[(sMM, tok_gstats)], inc=sDVE)
            g1 = S_.emit("vector",
                         lambda e: e.reduce_sum(out=r16[0:1, 16:32].unsqueeze(2),
                                                in_=ps_s2[0:1, :].rearrange(
                                                    "p (s c) -> p s c", c=128),
                                                axis=AX.X),
                         waits=[(sDVE, g0)], inc=sDVE)
            NG = float(128 * E)
            g2 = S_.emit("vector",
                         lambda e: e.tensor_scalar_mul(out=r16[0:1, 0:16],
                                                       in0=r16[0:1, 0:16],
                                                       scalar1=1.0 / NG),
                         waits=[(sDVE, g1)], inc=sDVE)
            g3 = S_.emit("vector",
                         lambda e: e.tensor_tensor(out=r16[0:1, 32:48],
                                                   in0=r16[0:1, 0:16],
                                                   in1=r16[0:1, 0:16], op=ALU.mult),
                         waits=[(sDVE, g2)], inc=sDVE)
            g4 = S_.emit("vector",
                         lambda e: e.scalar_tensor_tensor(
                             out=r16[0:1, 16:32], in0=r16[0:1, 16:32],
                             scalar=1.0 / NG, in1=r16[0:1, 32:48],
                             op0=ALU.mult, op1=ALU.subtract),
                         waits=[(sDVE, g3)], inc=sDVE)
            g5a = S_.emit("scalar",
                          lambda e: e.activation(out=r16[0:1, 16:32],
                                                 in_=r16[0:1, 16:32],
                                                 func=AF.Ln, bias=eps_b[0:1, :]),
                          waits=[(sDVE, g4)], inc=sACT)
            tr16 = S_.emit("scalar",
                           lambda e: e.activation(out=r16[0:1, 48:64],
                                                  in_=r16[0:1, 16:32],
                                                  func=AF.Exp, scale=-0.5),
                           waits=[(sACT, g5a)], inc=sACT)
            for b2 in range(16):
                S_.emit("vector",
                        lambda e, b2=b2: e.tensor_single_scalar(
                            out=mx[0:1, b2 * 128:(b2 + 1) * 128],
                            in_=onesrowf_s, scalar=r16[0:1, b2:b2 + 1], op=ALU.mult),
                        waits=[(sACT, tr16)] if b2 == 0 else (), inc=sDVE)
                S_.emit("vector",
                        lambda e, b2=b2: e.tensor_single_scalar(
                            out=rx[0:1, b2 * 128:(b2 + 1) * 128],
                            in_=onesrowf_s, scalar=r16[0:1, 48 + b2:49 + b2],
                            op=ALU.mult), inc=sDVE)
            S_.emit("vector",
                    lambda e: e.scalar_tensor_tensor(out=arow, in0=rx, scalar=0.2,
                                                     in1=gam_s, op0=ALU.mult,
                                                     op1=ALU.mult),
                    waits=[(dCONST, tok_gb)], inc=sDVE)
            S_.emit("vector",
                    lambda e: e.tensor_tensor(out=mx, in0=mx, in1=arow, op=ALU.mult),
                    inc=sDVE)
            tbr = S_.emit("vector",
                          lambda e: e.scalar_tensor_tensor(
                              out=brow, in0=bet_s, scalar=0.2, in1=mx,
                              op0=ALU.mult, op1=ALU.subtract), inc=sDVE)

        with ExitStack() as p35b:
            ps_bc2 = ps(p35b, "ps_bc2", [128, S])

            def bc2(e, src):
                ins = None
                for k in range(4):
                    ins = e.matmul(ps_bc2[:, k * 512:(k + 1) * 512], onesrowf_s,
                                   src[:, k * 512:(k + 1) * 512], start=True, stop=True)
                return ins
            t1 = S_.emit("tensor", lambda e: bc2(e, arow), waits=[(sDVE, tbr)], inc=sMM)
            t2 = S_.emit("scalar", lambda e: e.copy(out=A_b, in_=ps_bc2),
                         waits=[(sMM, t1)], inc=sACT)
            t3 = S_.emit("tensor", lambda e: bc2(e, brow), waits=[(sACT, t2)], inc=sMM)
            tBb = S_.emit("scalar", lambda e: e.copy(out=B_b, in_=ps_bc2),
                          waits=[(sMM, t3)], inc=sACT)
            tapply = None
            for t in range(NT):
                S_.emit("vector",
                        lambda e, t=t: e.tensor_tensor(
                            out=ysq[t % 2], in0=yn[:, t * S:(t + 1) * S],
                            in1=A_b, op=ALU.mult),
                        waits=[(sACT, tBb), (sMM, tok_gstats)], inc=sDVE)
                tapply = S_.emit(
                    "vector",
                    lambda e, t=t: e.tensor_tensor(out=yn[:, t * S:(t + 1) * S],
                                                   in0=ysq[t % 2], in1=B_b,
                                                   op=ALU.add), inc=sDVE)
            tok_yn = tapply
            if debug:
                S_.emit("sync", lambda e: e.dma_start(out=ytn_d, in_=yn),
                        waits=[(sDVE, tok_yn)], inc=dDBG)

        with ExitStack() as p4:
            wo_bf = [sb(p4, "wo0", [128, NT * 128], BF16),
                     sb(p4, "wo1", [128, NT * 128], BF16)]
            attn_sb = [sb(p4, "asb0", [128, S], BF16), sb(p4, "asb1", [128, S], BF16)]
            ps_wo = [ps(p4, "ps_wo0", [128, 512]), ps(p4, "ps_wo1", [128, 512])]
            womm = {}
            woev = {}
            wostore = {}
            for eo in range(16):
                w = [(sMM, womm[(eo - 2) * 4 + 3])] if eo >= 2 else []
                w.append((dWC, tok_wc))
                wtok = S_.emit("gpsimd",
                               lambda e, eo=eo: e.dma_start(out=wo_bf[eo % 2],
                                                            in_=wob_scr[eo]),
                               waits=w, inc=dWO[eo % 2])
                for it in range(4):
                    g = eo * 4 + it
                    def womf(e, eo=eo, it=it, g=g):
                        ins = None
                        for t in range(NT):
                            ins = e.matmul(ps_wo[g % 2],
                                           wo_bf[eo % 2][:, t * 128:(t + 1) * 128],
                                           yn[:, t * S + it * 512:t * S + (it + 1) * 512],
                                           start=(t == 0), stop=(t == NT - 1))
                        return ins
                    w = [(sDVE, tok_yn), (dWO[eo % 2], wtok)]
                    if g >= 2:
                        w.append((sACT, woev[g - 2]))
                    womm[g] = S_.emit("tensor", womf, waits=w, inc=sMM)
                    w = [(sMM, womm[g])]
                    if eo >= 2 and it == 0:
                        w.append((dAS[eo % 2], wostore[eo - 2]))
                    woev[g] = S_.emit(
                        "scalar",
                        lambda e, eo=eo, it=it, g=g: e.copy(
                            out=attn_sb[eo % 2][:, it * 512:(it + 1) * 512],
                            in_=ps_wo[g % 2]),
                        waits=w, inc=sACT)
                wostore[eo] = S_.emit(
                    "sync",
                    lambda e, eo=eo: e.dma_start(
                        out=attn_scr[eo * 128:(eo + 1) * 128, :], in_=attn_sb[eo % 2]),
                    waits=[(sACT, woev[eo * 4 + 3])], inc=dAS[eo % 2])
            tok_attn_stores = dict(wostore)

    with ExitStack() as p5:
        gated5 = sb(p5, "gated5", [128, 64 * 1024], BF16)
        wd_bf = [sb(p5, "wd0", [128, 64 * 128], BF16),
                 sb(p5, "wd1", [128, 64 * 128], BF16)]
        attn_t = [sb(p5, "at%d" % i, [128, 512], BF16) for i in range(4)]
        xt_t = [sb(p5, "xt%d" % i, [128, 512], F32) for i in range(4)]
        o1 = [sb(p5, "o1_0", [128, 512], F32), sb(p5, "o1_1", [128, 512], F32)]
        out_sb = [sb(p5, "ou0", [128, 512], F32), sb(p5, "ou1", [128, 512], F32)]
        ps_d = [ps(p5, "ps_d0", [128, 512]), ps(p5, "ps_d1", [128, 512]),
                ps(p5, "ps_d2", [128, 512]), ps(p5, "ps_d3", [128, 512])]
        wd3 = wdT.rearrange("(t p) c -> p t c", p=128)

        g5 = 0
        g_p5 = gate() + [(dAS[0], S_.cnt[dAS[0]]), (dAS[1], S_.cnt[dAS[1]])]
        first_gp5s = [True]
        first_gp5g = [True]
        dchain = {}
        dadd = {}
        ostore = {}
        for ic in range(2):
            gl_tok = None
            for ht in range(64):
                w = [(dGS[0], tok_gs[0]), (dGS[1], tok_gs[1])]
                if first_gp5s[0]:
                    w = list(g_p5) + w
                    first_gp5s[0] = False
                if ic == 1 and ht == 0:
                    w.append((sMM, dchain[31]))
                gl_tok = S_.emit(
                    "sync",
                    lambda e, ic=ic, ht=ht: e.dma_start(
                        out=gated5[:, ht * 1024:(ht + 1) * 1024],
                        in_=gated_scr[ht * 128:(ht + 1) * 128,
                                      ic * 1024:(ic + 1) * 1024]),
                    waits=w, inc=dG5)
            tok_gload = gl_tok
            for eo in range(16):
                w = [(sMM, dchain[g5 - 3])] if g5 >= 4 else []
                if first_gp5g[0]:
                    w = list(g_p5) + w
                    first_gp5g[0] = False
                w.append((dWC, tok_wc))
                wtok = S_.emit("gpsimd",
                               lambda e, eo=eo: e.dma_start(out=wd_bf[eo % 2],
                                                            in_=wdb_scr[eo]),
                               waits=w, inc=dWD[eo % 2])
                for is5 in range(2):
                    my = g5
                    g5 += 1
                    cols = ic * 1024 + is5 * 512
                    w = [(dAS[eo % 2], tok_attn_stores[eo])]
                    if my >= 4:
                        w.append((sDVE, dadd[my - 4]))
                    S_.emit("gpsimd",
                            lambda e, eo=eo, cols=cols, my=my: e.dma_start(
                                out=attn_t[my % 4],
                                in_=attn_scr[eo * 128:(eo + 1) * 128, cols:cols + 512]),
                            waits=w, inc=dT[my % 4])
                    w2 = [(sDVE, dadd[my - 4])] if my >= 4 else []
                    tl = S_.emit("sync",
                                 lambda e, eo=eo, cols=cols, my=my: e.dma_start(
                                     out=xt_t[my % 4],
                                     in_=xT[eo * 128:(eo + 1) * 128, cols:cols + 512]),
                                 waits=w2, inc=dT[my % 4])

                    def dmf(e, eo=eo, is5=is5, my=my):
                        ins = None
                        for t in range(64):
                            ins = e.matmul(
                                ps_d[my % 4],
                                wd_bf[eo % 2][:, t * 128:(t + 1) * 128],
                                gated5[:,
                                       t * 1024 + is5 * 512:t * 1024 + is5 * 512 + 512],
                                start=(t == 0), stop=(t == 63))
                        return ins
                    w = [(dG5, tok_gload), (dWD[eo % 2], wtok)]
                    if my >= 4:
                        w.append((sDVE, dadd[my - 4]))
                    dchain[my] = S_.emit("tensor", dmf, waits=w, inc=sMM)

                    def addf(e, my=my):
                        e.tensor_tensor(out=o1[my % 2], in0=ps_d[my % 4],
                                        in1=attn_t[my % 4], op=ALU.add)
                        return e.tensor_tensor(out=out_sb[my % 2], in0=o1[my % 2],
                                               in1=xt_t[my % 4], op=ALU.add)
                    w = [(sMM, dchain[my]), (dT[my % 4], tl)]
                    if my >= 2:
                        w.append((dOS[my % 2], ostore[my - 2]))
                    dadd[my] = S_.emit("vector", addf, waits=w, inc=sDVE)
                    ostore[my] = S_.emit(
                        "sync",
                        lambda e, eo=eo, cols=cols, my=my: e.dma_start(
                            out=yout[eo * 128:(eo + 1) * 128, cols:cols + 512],
                            in_=out_sb[my % 2]),
                        waits=[(sDVE, dadd[my])], inc=dOS[my % 2])
        final_waits = [(dOS[0], S_.cnt[dOS[0]]), (dOS[1], S_.cnt[dOS[1]])]
        if debug:
            final_waits.append((dDBG, S_.cnt[dDBG]))
        S_.emit("sync", lambda e: e.nop(), waits=final_waits)

    with nc.Block() as block:
        @block.sync
        def _(eng):
            S_.replay("sync", eng)

        @block.gpsimd
        def _(eng):
            S_.replay("gpsimd", eng)

        @block.tensor
        def _(eng):
            S_.replay("tensor", eng)

        @block.vector
        def _(eng):
            S_.replay("vector", eng)

        @block.scalar
        def _(eng):
            S_.replay("scalar", eng)

    stk.close()
    return nc


def kernel(L0_attn_x, L0_attn_w_qkv, L0_attn_wo, L0_attn_lambda_q1, L0_attn_lambda_q2,
           L0_attn_lambda_k1, L0_attn_lambda_k2, L0_attn_gamma, L0_attn_beta,
           L0_ffn_w_gate, L0_ffn_w_up, L0_ffn_w_down):
    x = np.asarray(L0_attn_x, dtype=np.float32)
    B = x.shape[0]
    xTh = np.ascontiguousarray(x.transpose(0, 2, 1))
    wqkvT = np.ascontiguousarray(np.asarray(L0_attn_w_qkv, np.float32).T)
    woT = np.ascontiguousarray(np.asarray(L0_attn_wo, np.float32).T)
    wgT = np.ascontiguousarray(np.asarray(L0_ffn_w_gate, np.float32).T)
    wuT = np.ascontiguousarray(np.asarray(L0_ffn_w_up, np.float32).T)
    wdT = np.ascontiguousarray(np.asarray(L0_ffn_w_down, np.float32).T)
    gam = np.ascontiguousarray(np.asarray(L0_attn_gamma, np.float32).reshape(1, S))
    bet = np.ascontiguousarray(np.asarray(L0_attn_beta, np.float32).reshape(1, S))
    lq1 = np.ascontiguousarray(np.asarray(L0_attn_lambda_q1, np.float32).reshape(1, HD))
    lq2 = np.ascontiguousarray(np.asarray(L0_attn_lambda_q2, np.float32).reshape(1, HD))
    lk1 = np.ascontiguousarray(np.asarray(L0_attn_lambda_k1, np.float32).reshape(1, HD))
    lk2 = np.ascontiguousarray(np.asarray(L0_attn_lambda_k2, np.float32).reshape(1, HD))

    nc = build_program()
    in_maps = []
    for c in range(B):
        in_maps.append({"xT": xTh[c], "wqkvT": wqkvT, "woT": woT, "wgT": wgT,
                        "wuT": wuT, "wdT": wdT, "gam": gam, "bet": bet,
                        "lq1": lq1, "lq2": lq2, "lk1": lk1, "lk2": lk2})
    res = run_bass_kernel_spmd(nc, in_maps, core_ids=list(range(8)))
    out = np.stack([np.asarray(res.results[c]["y"]) for c in range(B)], axis=0)
    return np.ascontiguousarray(out.transpose(0, 2, 1))
